# version 13
# baseline (speedup 1.0000x reference)
import numpy as np
import ml_dtypes
import concourse.bass as bass
import concourse.mybir as mybir
from concourse.bass_utils import run_bass_kernel_spmd

F32 = mybir.dt.float32
BF16 = mybir.dt.bfloat16
U8 = mybir.dt.uint8
AF = mybir.ActivationFunctionType
ALU = mybir.AluOpType

NCORE = 8
SEQ = 8192
E = 2048
OWN = 1024
HALO = 3072
NH = 16
NKV = 4
DH = 128
NI = 8192
WIN = 1024
CAP = 50.0
NEC = E // 128
NIC = NI // 128
NKT = HALO // 128
EXPB = -25.0

BPP = 212863


def _al(n, a=64):
    return (n + a - 1) // a * a


def _blocks(qc):
    Q0 = 1024 + qc * 512
    out = []
    for o in range(-WIN, WIN + 512 - 128 + 1, 128):
        js = [j for j in range(4) if abs(o - 128 * j) <= WIN]
        if not js:
            continue
        tris = [(j, +1) for j in js if o - 128 * j == WIN] + \
               [(j, -1) for j in js if o - 128 * j == -WIN]
        out.append((Q0 + o, js[0], js[-1], tris))
    assert len(out) == 20
    return out


class Ctr:
    def __init__(self, nc, name):
        self.s = nc.alloc_semaphore(name)
        self.n = 0

    def inc(self, k=1):
        self.n += k
        return (self.s, k)


class Sch:
    ENGS = ("pe", "act", "dve", "pool", "sync")

    def __init__(self, nc):
        self.nc = nc
        self.q = {e: [] for e in self.ENGS}
        self.hw = {e: {} for e in self.ENGS}

    def op(self, eng, meth, *args, wait=(), inc=None, **kw):
        real_waits = []
        for w in wait:
            if w is None:
                continue
            s, v = w
            if self.hw[eng].get(s.name, -1) < v:
                self.hw[eng][s.name] = v
                real_waits.append((s, v))
        if inc is not None:
            s, _ = inc
        def f(e, m=meth, a=args, k=kw, rw=tuple(real_waits), inc=inc):
            for (s, v) in rw:
                e.wait_ge(s, v)
            ins = getattr(e, m)(*a, **k)
            if inc is not None:
                ins.then_inc(inc[0], inc[1])
        self.q[eng].append(f)

    def raw(self, eng, fn):
        self.q[eng].append(fn)

    def barrier(self, ctrs):
        vals = {}
        for e in self.ENGS:
            c = ctrs[e]
            tgt = c.inc(1)
            vals[e] = (c.s, c.n)
            self.raw(e, lambda eng, s=tgt[0]: eng.drain().then_inc(s, 1))
        for e in self.ENGS:
            for o in self.ENGS:
                if o == e:
                    continue
                s, v = vals[o]
                if self.hw[e].get(s.name, -1) < v:
                    self.hw[e][s.name] = v
                    self.raw(e, lambda eng, s=s, v=v: eng.wait_ge(s, v))


def build_program(taps=()):
    nc = bass.Bass("TRN2", target_bir_lowering=False, debug=False,
                   detect_race_conditions=False)

    def din(name, shape, dt):
        return nc.dram_tensor(name, list(shape), dt, kind="ExternalInput").ap()

    xt = din("xt", (E, OWN), F32)
    cos_t = din("cos_t", (DH, OWN), F32)
    sin_t = din("sin_t", (DH, OWN), F32)
    kvout = nc.dram_tensor("kvout", [128, 8192], BF16, kind="Internal").ap()
    kvall = nc.dram_tensor("kvall", [8 * 128, 8192], BF16, kind="Internal").ap()
    zsel_d = din("zsel", (128, NKT * 31), BF16)
    tri_d = din("tri", (128, 256), BF16)
    rsel_d = din("rsel", (16, NH * 128), BF16)
    wq_d = din("wq_t", (NH, 128, NEC * 128), F32)
    wk_d = din("wk_t", (NKV, 128, NEC * 128), F32)
    wv_d = din("wv_t", (NEC, 128, 512), F32)
    wo_d = din("wo_t", (NEC, 128, NEC * 128), F32)
    gw_d = din("gw_t", (NIC, 128, NEC * 128), U8)
    uw_d = din("uw_t", (NIC, 128, NEC * 128), U8)
    dw_d = din("dw_t", (NEC, 128, NIC * 128), U8)
    gs_d = din("gs_t", (128, NIC), F32)
    us_d = din("us_t", (128, NIC), F32)
    ds_d = din("ds_t", (128, NEC), F32)
    qsc_d = din("qsc", (128, 1), F32)
    ksc_d = din("ksc", (128, 1), F32)

    out_d = nc.dram_tensor("out_t", [E, OWN], F32, kind="ExternalOutput").ap()
    x2k = "ExternalOutput" if "x2" in taps else "Internal"
    x2scr = nc.dram_tensor("x2scr", [E, OWN], F32, kind=x2k).ap()
    dbg = {}
    if "qkv" in taps:
        dbg["k"] = nc.dram_tensor("dbg_k", [128, NKV * HALO], BF16, kind="ExternalOutput").ap()
        dbg["v"] = nc.dram_tensor("dbg_v", [128, NKT * 512], BF16, kind="ExternalOutput").ap()
        dbg["q"] = nc.dram_tensor("dbg_q", [128, NH * OWN], BF16, kind="ExternalOutput").ap()
    if "attn" in taps:
        dbg["attn"] = nc.dram_tensor("dbg_attn", [128, NH * OWN], BF16, kind="ExternalOutput").ap()

    ARENA_B = (BPP - 256) // 64 * 64
    AR = nc.alloc_sbuf_tensor("arena", [128, ARENA_B], U8).ap()

    def view(off, parts, cols, dt):
        sz = {F32: 4, BF16: 2, U8: 1}[dt]
        nb = cols * sz
        assert off % 64 == 0 and off + nb <= ARENA_B, (off, nb)
        return AR[0:parts, off:off + nb].bitcast(dt)

    o = 0
    zsel = view(o, 128, NKT * 31, BF16); o = _al(o + NKT * 31 * 2)
    tri = view(o, 128, 256, BF16); o = _al(o + 512)
    rsel = view(o, 16, NH * 128, BF16); o = _al(o + NH * 128 * 2)
    gs_s = view(o, 128, NIC, F32); o = _al(o + NIC * 4)
    us_s = view(o, 128, NIC, F32); o = _al(o + NIC * 4)
    ds_s = view(o, 128, NEC, F32); o = _al(o + NEC * 4)
    qsc_s = view(o, 128, 1, F32); o = _al(o + 4)
    ksc_s = view(o, 128, 1, F32); o = _al(o + 4)
    neg25 = view(o, 128, 1, F32); o = _al(o + 4)
    eps12 = view(o, 128, 1, F32); o = _al(o + 4)
    eps5 = view(o, 128, 1, F32); o = _al(o + 4)
    ones_c = view(o, 128, 1, BF16); o = _al(o + 2)
    ones_r = view(o, 1, 128, BF16); o = _al(o + 256)
    P_END = o
    assert P_END < 8192

    a = P_END
    def Av(parts, cols, dt):
        nonlocal a
        v = view(a, parts, cols, dt)
        a = _al(a + cols * {F32: 4, BF16: 2, U8: 1}[dt])
        return v
    x_own = Av(128, NEC * OWN, BF16)
    xb = [Av(128, NEC * 512, BF16) for _ in range(2)]
    wpA = [Av(128, NEC * 128, BF16) for _ in range(2)]
    wvp = Av(128, NEC * 512, BF16)
    cosb = [Av(128, 512, F32) for _ in range(2)]
    sinb = [Av(128, 512, F32) for _ in range(2)]
    TOPS = 169984 + NH * OWN * 2
    rot_o = [a, TOPS]
    rot = [Av(128, 512, F32), view(TOPS, 128, 512, F32)]
    qa = [Av(128, 512, F32), view(TOPS + 2048, 128, 512, F32)]
    qr = [Av(128, 512, F32), view(TOPS + 4096, 128, 512, F32),
          view(TOPS + 6144, 128, 512, F32)]
    sqb = [view(rot_o[i], 128, 512, BF16) for i in range(2)]
    rnrow = [Av(1, 512, F32) for _ in range(2)]
    rnhi = [Av(1, 512, BF16) for _ in range(2)]
    rnlo = [Av(1, 512, BF16) for _ in range(2)]
    assert a <= 120832, a

    KO, VO, QO = 120832, 145408, 169984
    K_fm = view(KO, 128, NKV * HALO, BF16)
    V_tm = view(VO, 128, NKT * 512, BF16)
    Q_fm = view(QO, 128, NH * OWN, BF16)
    wkp = view(QO, 128, NKV * NEC * 128, BF16)
    assert QO + NH * OWN * 2 <= ARENA_B

    b = P_END
    def Bv(parts, cols, dt):
        nonlocal b
        v = view(b, parts, cols, dt)
        b = _al(b + cols * {F32: 4, BF16: 2, U8: 1}[dt])
        return v
    pt = [Bv(128, 1024, BF16) for _ in range(3)]
    th = [Bv(128, 1024, F32) for _ in range(3)]
    rzf = Bv(16, 512, F32)
    rzhi = Bv(16, 512, BF16)
    rzlo = Bv(16, 512, BF16)
    assert b <= 84992, b
    attn = view(84992, 128, NH * OWN, BF16)

    x2bf = view(P_END, 128, NEC * OWN, BF16)
    c = _al(P_END + NEC * OWN * 2)
    def Cv(parts, cols, dt):
        nonlocal c
        v = view(c, parts, cols, dt)
        c = _al(c + cols * {F32: 4, BF16: 2, U8: 1}[dt])
        return v
    wpC = [Cv(128, NEC * 128, BF16) for _ in range(2)]
    xres = [Cv(128, 512, F32) for _ in range(4)]
    rbsb = Cv(128, 512, F32)
    mrbsb = Cv(128, 512, F32)
    sqc = [Cv(128, 512, BF16) for _ in range(4)]
    ybfr = [Cv(128, 512, BF16) for _ in range(4)]
    meanr = Cv(1, 512, F32)
    varr = Cv(1, 512, F32)
    rstdr = Cv(1, 512, F32)
    mrr = Cv(1, 512, F32)
    shi = Cv(1, 512, BF16)
    slo = Cv(1, 512, BF16)
    mhi = Cv(1, 512, BF16)
    mlo = Cv(1, 512, BF16)
    assert c <= 84992, c
    yC = view(120832, 128, NEC * 512, F32)

    d = _al(P_END + NEC * OWN * 2)
    def Dv(parts, cols, dt):
        nonlocal d
        v = view(d, parts, cols, dt)
        d = _al(d + cols * {F32: 4, BF16: 2, U8: 1}[dt])
        return v
    dwp = [Dv(128, NIC * 128, BF16) for _ in range(2)]
    outs = [Dv(128, 512, F32) for _ in range(2)]
    x2r = [Dv(128, 512, F32) for _ in range(2)]
    xsb = Dv(128, 512, F32)
    xsb2 = Dv(128, 512, F32)
    tsc1 = Dv(128, 512, F32)
    tsc2 = Dv(128, 512, F32)
    sgs = Dv(128, 512, F32)
    y2bfr = [Dv(128, 512, BF16) for _ in range(2)]
    mean2 = Dv(1, 512, F32)
    var2 = Dv(1, 512, F32)
    rstd2 = Dv(1, 512, F32)
    mr2 = Dv(1, 512, F32)
    shi2 = Dv(1, 512, BF16)
    slo2 = Dv(1, 512, BF16)
    mhi2 = Dv(1, 512, BF16)
    mlo2 = Dv(1, 512, BF16)
    rb2 = Dv(128, 512, F32)
    mrb2 = Dv(128, 512, F32)
    sq2 = [Dv(128, 512, BF16) for _ in range(2)]
    GUP_O = _al(d)
    assert GUP_O + 32 * 1024 + NIC * 512 * 2 <= ARENA_B, GUP_O
    gup = [view(GUP_O + i * 8192, 128, 2 * NEC * 128, BF16) for i in range(4)]
    y2a = view(GUP_O, 128, 8 * 512, F32)
    y2b = view(GUP_O + 16384, 128, 8 * 512, F32)
    HS_O = GUP_O + 32768
    hS = view(HS_O, 128, NIC * 512, BF16)

    def y2t(ce):
        return (y2a if ce < 8 else y2b)[:, (ce % 8) * 512:(ce % 8) * 512 + 512]

    PS = nc.alloc_psum_tensor("PS", [128, 4096], F32).ap()

    def bank(i, n=1, parts=128):
        return PS[0:parts, i * 512:(i + n) * 512]

    S = Sch(nc)
    dW = Ctr(nc, "dW")
    dX = Ctr(nc, "dX")
    dS = Ctr(nc, "dS")
    cPE = Ctr(nc, "cPE")
    cACT = Ctr(nc, "cACT")
    cDVE = Ctr(nc, "cDVE")
    cPOOL = Ctr(nc, "cPOOL")
    cSYNC = Ctr(nc, "cSYNC")
    CTRS = {"pe": cPE, "act": cACT, "dve": cDVE, "pool": cPOOL, "sync": cSYNC}

    S.op("sync", "dma_start", out=zsel, in_=zsel_d, inc=dX.inc(16))
    S.op("sync", "dma_start", out=tri, in_=tri_d, inc=dX.inc(16))
    S.op("sync", "dma_start", out=rsel, in_=rsel_d, inc=dX.inc(16))
    S.op("sync", "dma_start", out=gs_s, in_=gs_d, inc=dX.inc(16))
    S.op("sync", "dma_start", out=us_s, in_=us_d, inc=dX.inc(16))
    S.op("sync", "dma_start", out=ds_s, in_=ds_d, inc=dX.inc(16))
    S.op("sync", "dma_start", out=qsc_s, in_=qsc_d, inc=dX.inc(16))
    S.op("sync", "dma_start", out=ksc_s, in_=ksc_d, inc=dX.inc(16))
    preload_dx = dX.n
    S.op("dve", "memset", neg25, EXPB)
    S.op("dve", "memset", eps12, 1e-12)
    S.op("dve", "memset", eps5, 1e-5)
    S.op("dve", "memset", ones_c, 1.0)
    S.op("dve", "memset", ones_r, 1.0, inc=cDVE.inc())
    m_pre_dve = cDVE.n

    def xv(cin, T2):
        return x_own[:, cin * OWN + T2 * 512: cin * OWN + T2 * 512 + 512]

    frees = {}

    chain_i = [0]

    def chain_front(ps_pp, cosv, sinv, m_proj):
        ci = chain_i[0]
        chain_i[0] += 1
        s = ci % 2
        s3 = ci % 3
        ps = bank(ps_pp)
        S.op("act", "activation", rot[s][0:64, :], ps[64:128, :], AF.Copy, scale=-1.0,
             wait=[(cPE.s, m_proj), frees.get(("rot", s)), frees.get(("sqb", s))])
        S.op("act", "activation", rot[s][64:128, :], ps[0:64, :], AF.Copy,
             inc=cACT.inc())
        m_rot = cACT.n
        S.op("dve", "tensor_mul", qa[s], ps, cosv, wait=[(cACT.s, m_rot)])
        S.op("dve", "tensor_mul", rot[s], rot[s], sinv)
        S.op("dve", "tensor_add", qr[s3], qa[s], rot[s],
             wait=[frees.get(("qr", s3))])
        S.op("dve", "tensor_mul", sqb[s], qr[s3], qr[s3],
             inc=cDVE.inc())
        m_qr = cDVE.n
        frees[("rot", s)] = (cDVE.s, m_qr)
        return dict(s=s, s3=s3, r=s % 2, m_qr=m_qr)

    def chain_tail1(ctx):
        s, r, m_qr = ctx["s"], ctx["r"], ctx["m_qr"]
        S.op("pe", "matmul", bank(3, parts=1), ones_c, sqb[s], start=True, stop=True,
             wait=[(cDVE.s, m_qr), frees.get("b3")], inc=cPE.inc())
        m_ss = cPE.n
        frees[("sqb", s)] = (cPE.s, m_ss)
        S.op("act", "activation", rnrow[r], bank(3, parts=1), AF.Ln,
             bias=eps12[0:1, :],
             wait=[(cPE.s, m_ss), frees.get(("rnrow", r))])
        S.op("act", "activation", rnrow[r], rnrow[r], AF.Exp, scale=-0.5,
             inc=cACT.inc())
        m_rn = cACT.n
        frees["b3"] = (cACT.s, m_rn)
        S.op("dve", "tensor_copy", rnhi[r], rnrow[r],
             wait=[(cACT.s, m_rn), frees.get(("rows", r))])
        S.op("dve", "tensor_sub", rnlo[r], rnrow[r], rnhi[r], inc=cDVE.inc())
        m_rows = cDVE.n
        frees[("rnrow", r)] = (cDVE.s, m_rows)
        ctx["m_rows"] = m_rows

    def chain_tail2(ctx, sc_ap, dest):
        s3, r, m_rows = ctx["s3"], ctx["r"], ctx["m_rows"]
        S.op("pe", "matmul", bank(2), ones_r, rnhi[r], start=True, stop=False,
             wait=[(cDVE.s, m_rows), frees.get("b2")])
        S.op("pe", "matmul", bank(2), ones_r, rnlo[r], start=False, stop=True,
             inc=cPE.inc())
        m_r1 = cPE.n
        frees[("rows", r)] = (cPE.s, m_r1)
        S.op("dve", "scalar_tensor_tensor", dest, qr[s3], sc_ap, bank(2),
             ALU.mult, ALU.mult, wait=[(cPE.s, m_r1)], inc=cDVE.inc())
        m_fin = cDVE.n
        frees[("qr", s3)] = (cDVE.s, m_fin)
        frees["b2"] = (cDVE.s, m_fin)
        return m_fin


    nproj = [0]
    projfree = {}

    def do_proj(lhs_tiles, rhs_tiles, extra_wait=()):
        pp = nproj[0] % 2
        nproj[0] += 1
        w = [projfree.get(pp)] + list(extra_wait)
        for ci in range(NEC):
            S.op("pe", "matmul", bank(pp), lhs_tiles(ci), rhs_tiles(ci),
                 start=(ci == 0), stop=(ci == NEC - 1),
                 wait=w if ci == 0 else (),
                 inc=cPE.inc() if ci == NEC - 1 else None)
        return pp, cPE.n

    cc = Ctr(nc, "cc")
    xt_pc = xt.rearrange("(c p) t -> p c t", p=128)
    pend = []

    def _advance(new):
        pend.append(new)
        if len(pend) >= 2:
            c = pend[-2]
            if "m_rows" not in c[0]:
                chain_tail1(c[0])
        if len(pend) >= 3:
            c = pend.pop(0)
            chain_tail2(c[0], c[1], c[2])

    def _flush_all():
        while pend:
            c = pend.pop(0)
            if "m_rows" not in c[0]:
                chain_tail1(c[0])
            chain_tail2(c[0], c[1], c[2])

    for g in range(NKV):
        S.op("dve", "memset", K_fm[:, g * HALO: g * HALO + 1024], 0.0)
        S.op("dve", "memset", K_fm[:, g * HALO + 2048: g * HALO + 3072], 0.0)
    S.op("dve", "memset", V_tm[:, 0:8 * 512], 0.0)
    S.op("dve", "memset", V_tm[:, 16 * 512:24 * 512], 0.0, inc=cDVE.inc())
    m_hzero = cDVE.n

    S.op("pool", "dma_start", out=x_own.rearrange("p (c t) -> p c t", c=NEC),
         in_=xt_pc, inc=dW.inc(16))
    x_dw = dW.n
    S.op("pool", "dma_start", out=wkp.rearrange("p (g c) -> p g c", g=NKV),
         in_=wk_d.transpose([1, 0, 2]), inc=dW.inc(16))
    wk_dw = dW.n
    S.op("pool", "dma_start", out=wvp.rearrange("p (g c) -> p g c", g=NEC),
         in_=wv_d.transpose([1, 0, 2]), inc=dW.inc(16))
    wv_dw = dW.n
    for i in range(2):
        S.op("sync", "dma_start", out=cosb[i], in_=cos_t[:, i * 512:(i + 1) * 512],
             inc=dX.inc(16))
        S.op("sync", "dma_start", out=sinb[i], in_=sin_t[:, i * 512:(i + 1) * 512],
             inc=dX.inc(16))
    cs_dx = dX.n

    for T2 in range(2):
        for g in range(NKV):
            pp, m_proj = do_proj(
                lambda ci, g=g: wkp[:, g * 2048 + ci * 128: g * 2048 + ci * 128 + 128],
                lambda ci, T2=T2: xv(ci, T2),
                extra_wait=[(dW.s, max(x_dw, wk_dw)), (dX.s, cs_dx),
                            (dX.s, preload_dx), (cDVE.s, m_pre_dve)])
            ctx = chain_front(pp, cosb[T2], sinb[T2], m_proj)
            projfree[pp] = (cDVE.s, ctx["m_qr"])
            _advance((ctx, ksc_s,
                      K_fm[:, g * HALO + 1024 + T2 * 512:
                           g * HALO + 1024 + T2 * 512 + 512]))
        for sub in range(4):
            pp = 4 + (sub % 2)
            for ci in range(NEC):
                S.op("pe", "matmul", bank(pp),
                     xv(ci, T2)[:, sub * 128:sub * 128 + 128],
                     wvp[:, ci * 512:(ci + 1) * 512],
                     start=(ci == 0), stop=(ci == NEC - 1),
                     wait=[frees.get(("bv", pp)), (dW.s, wv_dw)] if ci == 0 else (),
                     inc=cPE.inc() if ci == NEC - 1 else None)
            m_v = cPE.n
            kt = 8 + T2 * 4 + sub
            S.op("dve", "tensor_copy", V_tm[:, kt * 512:(kt + 1) * 512], bank(pp),
                 wait=[(cPE.s, m_v)], inc=cDVE.inc())
            frees[("bv", pp)] = (cDVE.s, cDVE.n)
    _flush_all()
    m_kv = cDVE.n

    for g in range(NKV):
        S.op("sync", "dma_start", out=kvout[:, g * 1024:(g + 1) * 1024],
             in_=K_fm[:, g * HALO + 1024: g * HALO + 2048],
             wait=[(cDVE.s, m_kv)], inc=dS.inc(16))
    S.op("sync", "dma_start", out=kvout[:, 4096:8192],
         in_=V_tm[:, 8 * 512:16 * 512], inc=dS.inc(16))
    exp_ds = dS.n

    wp_free = {}
    for h in range(NH):
        hp = h % 2
        S.op("pool", "dma_start", out=wpA[hp], in_=wq_d[h],
             wait=[wp_free.get(hp)], inc=dW.inc(16))
        wq_dw = dW.n
        if h == 2:
            def _cc(e, exp_ds=exp_ds):
                e.wait_ge(dS.s, exp_ds)
                e.collective_compute(
                    "AllGather", ALU.bypass,
                    replica_groups=[list(range(NCORE))],
                    ins=[kvout], outs=[kvall],
                ).then_inc(cc.s, 1)
            S.raw("pool", _cc)
            cc.n += 1
        for t2 in range(2):
            pp, m_proj = do_proj(
                lambda ci, hp=hp: wpA[hp][:, ci * 128:(ci + 1) * 128],
                lambda ci, t2=t2: x_own[:, ci * OWN + t2 * 512: ci * OWN + t2 * 512 + 512],
                extra_wait=[(dW.s, wq_dw)])
            ctx = chain_front(pp, cosb[t2], sinb[t2], m_proj)
            projfree[pp] = (cDVE.s, ctx["m_qr"])
            _advance((ctx, qsc_s,
                      Q_fm[:, h * OWN + t2 * 512: h * OWN + t2 * 512 + 512]))
        wp_free[hp] = (cPE.s, cPE.n)
    _flush_all()

    def _imports(e):
        e.wait_ge(cc.s, 1)
        pid = nc.partition_id(engines=[e.engine])
        for side, base in ((-1, 0), (+1, 2048)):
            row = (pid + side) * 128
            for g in range(NKV):
                e.dma_start(
                    out=K_fm[:, g * HALO + base: g * HALO + base + 1024],
                    in_=kvall[bass.ds(row, 128), g * 1024:(g + 1) * 1024],
                    bounds_check="skip_entire_dma",
                ).then_inc(dX.s, 16)
            voff = 0 if side < 0 else 16 * 512
            e.dma_start(
                out=V_tm[:, voff: voff + 8 * 512],
                in_=kvall[bass.ds(row, 128), 4096:8192],
                bounds_check="skip_entire_dma",
            ).then_inc(dX.s, 16)
    S.raw("sync", _imports)
    dX.n += 10 * 16
    S.raw("sync", lambda e, v=dX.n: e.wait_ge(dX.s, v))

    S.barrier(CTRS)

    if "qkv" in taps:
        S.op("sync", "dma_start", out=dbg["k"], in_=K_fm, inc=dS.inc(16))
        S.op("sync", "dma_start", out=dbg["v"], in_=V_tm, inc=dS.inc(16))
        S.op("sync", "dma_start", out=dbg["q"], in_=Q_fm, inc=dS.inc(16))

    st_free = {}
    pt_free = {}
    th_free = {}
    pv_free = {}
    nrm_free = None
    z_read = None
    post = []

    for qc in range(2):
        blocks = _blocks(qc)
        groups = [blocks[i:i + 2] for i in range(0, 20, 2)]
        z_started = [False]
        nz_total = NH * 20
        nz = [0]
        flat = [(h, gi, grp) for h in range(NH) for gi, grp in enumerate(groups)]
        pv_cnt = {h: 0 for h in range(NH)}
        ginfo = {}

        def emit_pvz(idx, flat=flat, pv_cnt=pv_cnt, ginfo=ginfo,
                     z_started=z_started, nz=nz, nz_total=nz_total, qc=qc):
            h, gi, grp = flat[idx]
            g = h // 4
            hv = 4 + h % 2
            p, m_exp, m_tri = ginfo.pop(idx)
            for bi, (K0, jlo, jhi, tris) in enumerate(grp):
                kt = K0 // 128
                n0, n1 = jlo * 128, (jhi + 1) * 128
                rhs = pt[p][:, bi * 512 + n0: bi * 512 + n1]
                w = [(cACT.s, m_exp)]
                if tris:
                    w.append((cDVE.s, m_tri))
                if pv_cnt[h] == 0:
                    w.append(pv_free.get(hv))
                S.op("pe", "matmul", bank(hv)[:, n0:n1],
                     V_tm[:, kt * 512 + g * 128: kt * 512 + g * 128 + 128],
                     rhs, start=(pv_cnt[h] == 0), stop=(pv_cnt[h] == 19),
                     skip_group_check=True, wait=w)
                pv_cnt[h] += 1
                zw = []
                if not z_started[0]:
                    zw = [z_read] if z_read else []
                nz[0] += 1
                S.op("pe", "matmul", bank(7, parts=16)[:, n0:n1],
                     zsel[:, kt * 31 + (15 - h): kt * 31 + (15 - h) + 16],
                     rhs, start=(not z_started[0]), stop=(nz[0] == nz_total),
                     skip_group_check=True, wait=zw,
                     inc=cPE.inc())
                z_started[0] = True
                pt_free[p] = (cPE.s, cPE.n)
            if pv_cnt[h] == 20:
                S.op("dve", "tensor_copy",
                     attn[:, h * OWN + qc * 512: h * OWN + qc * 512 + 512],
                     bank(hv), wait=[(cPE.s, cPE.n)], inc=cDVE.inc())
                pv_free[hv] = (cDVE.s, cDVE.n)

        for idx, (h, gi, grp) in enumerate(flat):
            g = h // 4
            p = idx % 2
            p3 = idx % 3
            gsz = len(grp)
            a0 = grp[0][1] * 128
            a1 = (gsz - 1) * 512 + (grp[-1][2] + 1) * 128
            for bi, (K0, jlo, jhi, tris) in enumerate(grp):
                n0, n1 = jlo * 128, (jhi + 1) * 128
                w = [st_free.get(p)] if bi == 0 else ()
                S.op("pe", "matmul",
                     PS[:, (2 * p + bi) * 512 + n0: (2 * p + bi) * 512 + n1],
                     K_fm[:, g * HALO + K0: g * HALO + K0 + 128],
                     Q_fm[:, h * OWN + qc * 512 + n0: h * OWN + qc * 512 + n1],
                     start=True, stop=True, wait=w,
                     inc=cPE.inc() if bi == gsz - 1 else None)
            m_sc = cPE.n
            S.op("act", "activation", th[p3][:, a0:a1],
                 PS[:, 2 * p * 512 + a0: 2 * p * 512 + a1], AF.Tanh,
                 scale=1.0 / (np.sqrt(DH) * CAP),
                 wait=[(cPE.s, m_sc), th_free.get(p3)], inc=cACT.inc())
            st_free[p] = (cACT.s, cACT.n)
            S.op("act", "activation", pt[p3][:, a0:a1], th[p3][:, a0:a1],
                 AF.Exp, scale=CAP, bias=neg25,
                 wait=[pt_free.get(p3)], inc=cACT.inc())
            m_exp = cACT.n
            th_free[p3] = (cACT.s, m_exp)
            m_tri = cDVE.n
            ntri = sum(len(t[3]) for t in grp)
            if ntri:
                cnt = 0
                for bi, (K0, jlo, jhi, tris) in enumerate(grp):
                    for (j, sign) in tris:
                        cnt += 1
                        sl = pt[p3][:, bi * 512 + j * 128: bi * 512 + j * 128 + 128]
                        tm = tri[:, 0:128] if sign > 0 else tri[:, 128:256]
                        S.op("dve", "tensor_mul", sl, sl, tm,
                             wait=[(cACT.s, m_exp)],
                             inc=cDVE.inc() if cnt == ntri else None)
                m_tri = cDVE.n
            ginfo[idx] = (p3, m_exp, m_tri)
            if idx >= 2:
                emit_pvz(idx - 2)
            if post:
                post.pop(0)()
        emit_pvz(len(flat) - 2)
        emit_pvz(len(flat) - 1)
        while post:
            post.pop(0)()
        m_z = cPE.n
        S.op("dve", "reciprocal", rzf, bank(7, parts=16), wait=[(cPE.s, m_z)])
        S.op("dve", "tensor_copy", rzhi, rzf)
        S.op("dve", "tensor_sub", rzlo, rzf, rzhi, inc=cDVE.inc())
        m_rz = cDVE.n
        z_read = (cDVE.s, m_rz)

        def mk_norm(h, qc=qc, m_rz=m_rz):
            def go():
                nonlocal nrm_free
                S.op("pe", "matmul", bank(6), rsel[:, h * 128:(h + 1) * 128],
                     rzhi, start=True, stop=False,
                     wait=[(cDVE.s, m_rz), nrm_free,
                           pv_free.get(4 + h % 2)])
                S.op("pe", "matmul", bank(6), rsel[:, h * 128:(h + 1) * 128],
                     rzlo, start=False, stop=True, inc=cPE.inc())
                m_r1 = cPE.n
                sl = attn[:, h * OWN + qc * 512: h * OWN + qc * 512 + 512]
                S.op("dve", "tensor_mul", sl, sl, bank(6),
                     wait=[(cPE.s, m_r1)], inc=cDVE.inc())
                nrm_free = (cDVE.s, cDVE.n)
            return go
        for h in range(NH):
            post.append(mk_norm(h))
    while post:
        post.pop(0)()

    S.barrier(CTRS)
    if "attn" in taps:
        S.op("sync", "dma_start", out=dbg["attn"], in_=attn, inc=dS.inc(16))

    def ln_stats_tail(sbank, qbank, m_sums, rows, bvec):
        meanv, varv, rstdv, mrv, hiv, lov, mhiv, mlov = rows
        rbv, mrbv, bb1, bb2 = bvec
        S.op("dve", "tensor_scalar_mul", meanv, bank(sbank, parts=1), 1.0 / E,
             wait=[(cPE.s, m_sums)])
        S.op("dve", "tensor_mul", varv, meanv, meanv)
        S.op("dve", "scalar_tensor_tensor", varv, bank(qbank, parts=1), 1.0 / E,
             varv, ALU.mult, ALU.subtract, inc=cDVE.inc())
        m_var = cDVE.n
        S.op("act", "activation", rstdv, varv, AF.Ln, bias=eps5[0:1, :],
             wait=[(cDVE.s, m_var)])
        S.op("act", "activation", rstdv, rstdv, AF.Exp, scale=-0.5,
             inc=cACT.inc())
        m_rstd = cACT.n
        S.op("dve", "tensor_mul", mrv, meanv, rstdv, wait=[(cACT.s, m_rstd)])
        S.op("dve", "tensor_copy", hiv, rstdv)
        S.op("dve", "tensor_sub", lov, rstdv, hiv)
        S.op("dve", "tensor_copy", mhiv, mrv)
        S.op("dve", "tensor_sub", mlov, mrv, mhiv, inc=cDVE.inc())
        m_rows2 = cDVE.n
        S.op("pe", "matmul", bank(bb1), ones_r, hiv, start=True, stop=False,
             wait=[(cDVE.s, m_rows2)])
        S.op("pe", "matmul", bank(bb1), ones_r, lov, start=False, stop=True)
        S.op("pe", "matmul", bank(bb2), ones_r, mhiv, start=True, stop=False)
        S.op("pe", "matmul", bank(bb2), ones_r, mlov, start=False, stop=True,
             inc=cPE.inc())
        m_bc = cPE.n
        S.op("dve", "tensor_copy", rbv, bank(bb1), wait=[(cPE.s, m_bc)])
        S.op("dve", "tensor_copy", mrbv, bank(bb2), inc=cDVE.inc())
        return cDVE.n

    wpc_free = {}
    xres_free = {}
    m_x2bf = {}
    gup_free = {}
    dwp_free = {}
    yB = [yC, view(153600, 128, NEC * 512, F32)]

    def c_stats(ce, qc):
        m_y = frees[("my", ce, qc)]
        S.op("pe", "matmul", bank(2 + 2 * qc, parts=1), ones_c,
             ybfr[2 * qc + ce % 2],
             start=(ce == 0), stop=(ce == NEC - 1),
             wait=[m_y])
        S.op("pe", "matmul", bank(3 + 2 * qc, parts=1), ones_c,
             sqc[2 * qc + ce % 2],
             start=(ce == 0), stop=(ce == NEC - 1),
             inc=cPE.inc())
        frees[("ybf", qc, ce % 2)] = (cPE.s, cPE.n)
        frees[("sqc", qc, ce % 2)] = (cPE.s, cPE.n)

    for ce in range(NEC):
        cp = ce % 2
        S.op("pool", "dma_start", out=wpC[cp], in_=wo_d[ce],
             wait=[wpc_free.get(cp)], inc=dW.inc(16))
        wo_dw = dW.n
        for qc in range(2):
            xr = xres[2 * qc + ce % 2]
            S.op("sync", "dma_start", out=xr,
                 in_=xt[ce * 128:(ce + 1) * 128, qc * 512:(qc + 1) * 512],
                 wait=[xres_free.get((qc, ce % 2))], inc=dX.inc(16))
            xr_dx = dX.n
            for ch in range(NEC):
                S.op("pe", "matmul", bank(qc),
                     wpC[cp][:, ch * 128:(ch + 1) * 128],
                     attn[:, ch * OWN + qc * 512: ch * OWN + qc * 512 + 512],
                     start=(ch == 0), stop=(ch == NEC - 1),
                     wait=[(dW.s, wo_dw), frees.get(("cb", qc))] if ch == 0 else (),
                     inc=cPE.inc() if ch == NEC - 1 else None)
            m_wo = cPE.n
            yv = yB[qc][:, ce * 512:(ce + 1) * 512]
            S.op("dve", "tensor_add", yv, bank(qc), xr,
                 wait=[(cPE.s, m_wo), (dX.s, xr_dx)])
            frees[("cb", qc)] = (cDVE.s, cDVE.n + 1)
            S.op("dve", "tensor_copy", ybfr[2 * qc + ce % 2], yv,
                 wait=[frees.get(("ybf", qc, ce % 2))])
            S.op("dve", "tensor_mul", sqc[2 * qc + ce % 2], ybfr[2 * qc + ce % 2],
                 ybfr[2 * qc + ce % 2],
                 wait=[frees.get(("sqc", qc, ce % 2))], inc=cDVE.inc())
            frees[("my", ce, qc)] = (cDVE.s, cDVE.n)
            xres_free[(qc, ce % 2)] = (cDVE.s, cDVE.n)
            if ce >= 1:
                c_stats(ce - 1, qc)
        wpc_free[cp] = (cPE.s, cPE.n)
    for qc in range(2):
        c_stats(NEC - 1, qc)
    m_sums = cPE.n
    x2s_chunk = {}
    for qc in range(2):
        bb = (6, 7) if qc == 0 else (4, 5)
        m_bcast = ln_stats_tail(2 + 2 * qc, 3 + 2 * qc, m_sums,
                                (meanr, varr, rstdr, mrr, shi, slo, mhi, mlo),
                                (rbsb, mrbsb, bb[0], bb[1]))
        if qc == 0:
            frees["b6d"] = (cDVE.s, m_bcast)
            frees["b7d"] = (cDVE.s, m_bcast)
        for ce in range(NEC):
            yv = yB[qc][:, ce * 512:(ce + 1) * 512]
            S.op("dve", "scalar_tensor_tensor", yv, yv, 1.0, rbsb,
                 ALU.mult, ALU.mult, wait=[(cDVE.s, m_bcast)])
            S.op("dve", "tensor_sub", yv, yv, mrbsb)
            S.op("dve", "tensor_copy",
                 x2bf[:, ce * OWN + qc * 512: ce * OWN + qc * 512 + 512], yv,
                 inc=cDVE.inc())
            m_x2 = cDVE.n
            S.op("sync", "dma_start",
                 out=x2scr[ce * 128:(ce + 1) * 128, qc * 512:(qc + 1) * 512],
                 in_=yv, wait=[(cDVE.s, m_x2)], inc=dS.inc(16))
        m_x2bf[qc] = cDVE.n
        x2s_chunk[qc] = dS.n
    x2s_done = dS.n

    frees[("dg", 0)] = frees.get(("cb", 0))
    frees[("dg", 1)] = frees.get(("cb", 1))
    frees[("dg", 2)] = (cDVE.s, m_x2bf[0])
    frees[("dg", 3)] = (cDVE.s, m_x2bf[0])
    for _i in range(4):
        gup_free[_i] = (dS.s, x2s_chunk[0])
    gup_free["pe"] = (cPE.s, m_sums)
    dwp_free[0] = (cDVE.s, m_x2bf[1])
    dwp_free[1] = (cDVE.s, m_x2bf[1])
    frees["hfree"] = (dS.s, x2s_done)

    x2r_free = {}
    outs_free = {}
    for qc in range(2):
        for ce in range(NEC):
            S.op("pe", "matmul", bank(6, parts=1), ones_c,
                 x2bf[:, ce * OWN + qc * 512: ce * OWN + qc * 512 + 512],
                 start=(ce == 0), stop=(ce == NEC - 1),
                 wait=[frees.get("b6d"), (cDVE.s, m_x2bf[qc])] if ce == 0 else (),
                 inc=cPE.inc() if ce == NEC - 1 else None)
        m_xs = cPE.n
        S.op("dve", "tensor_scalar_mul", mean2, bank(6, parts=1), -128.0,
             wait=[(cPE.s, m_xs)])
        S.op("dve", "tensor_copy", shi2, mean2)
        S.op("dve", "tensor_sub", slo2, mean2, shi2, inc=cDVE.inc())
        m_hl = cDVE.n
        S.op("pe", "matmul", bank(7), ones_r, shi2, start=True, stop=False,
             wait=[(cDVE.s, m_hl), frees.get("b7d")])
        S.op("pe", "matmul", bank(7), ones_r, slo2, start=False, stop=True,
             inc=cPE.inc())
        m_bc = cPE.n
        S.op("dve", "tensor_copy", xsb, bank(7), wait=[(cPE.s, m_bc), frees.get("xsb")],
             inc=cDVE.inc())
        m_xsb = cDVE.n
        frees["b6d"] = (cDVE.s, m_xsb)
        frees["b7d"] = (cDVE.s, m_xsb)

        for ic in range(NIC):
            ip = (ic // 2) % 2
            sub = (ic % 2) * 2048
            if ic % 2 == 0:
                S.op("pool", "dma_start",
                     out=gup[ip].rearrange("p (g c) -> p g c", g=2),
                     in_=gw_d[ic:ic + 2].transpose([1, 0, 2]),
                     wait=[gup_free.get(ip), gup_free.get("pe")], inc=dW.inc(16))
                S.op("pool", "dma_start",
                     out=gup[2 + ip].rearrange("p (g c) -> p g c", g=2),
                     in_=uw_d[ic:ic + 2].transpose([1, 0, 2]),
                     wait=[gup_free.get(2 + ip)], inc=dW.inc(16))
            w_dw = dW.n
            bg, bu = ic % 2, 2 + ic % 2
            for ch in range(NEC):
                S.op("pe", "matmul", bank(bg),
                     gup[ip][:, sub + ch * 128: sub + (ch + 1) * 128],
                     x2bf[:, ch * OWN + qc * 512: ch * OWN + qc * 512 + 512],
                     start=(ch == 0), stop=(ch == NEC - 1),
                     wait=[(dW.s, w_dw), frees.get(("dg", bg)),
                           frees.get(("db", bg))] if ch == 0 else ())
            for ch in range(NEC):
                S.op("pe", "matmul", bank(bu),
                     gup[2 + ip][:, sub + ch * 128: sub + (ch + 1) * 128],
                     x2bf[:, ch * OWN + qc * 512: ch * OWN + qc * 512 + 512],
                     start=(ch == 0), stop=(ch == NEC - 1),
                     wait=[frees.get(("dg", bu)),
                           frees.get(("db", bu))] if ch == 0 else (),
                     inc=cPE.inc() if ch == NEC - 1 else None)
            m_gu = cPE.n
            S.op("dve", "tensor_add", tsc1, bank(bg), xsb,
                 wait=[(cPE.s, m_gu), (cDVE.s, m_xsb), frees.get("tsc1")])
            S.op("dve", "tensor_add", tsc2, bank(bu), xsb,
                 wait=[frees.get("tsc2")], inc=cDVE.inc())
            m_t = cDVE.n
            frees[("dg", bg)] = (cDVE.s, m_t)
            frees[("dg", bu)] = (cDVE.s, m_t)
            if ic % 2 == 1:
                gup_free[ip] = (cPE.s, m_gu)
                gup_free[2 + ip] = (cPE.s, m_gu)
            S.op("act", "activation", sgs, tsc1, AF.Silu,
                 scale=gs_s[:, ic:ic + 1],
                 wait=[(cDVE.s, m_t), frees.get("sgs")], inc=cACT.inc())
            m_sg = cACT.n
            frees["tsc1"] = (cACT.s, m_sg)
            S.op("dve", "scalar_tensor_tensor", hS[:, ic * 512:(ic + 1) * 512],
                 tsc2, us_s[:, ic:ic + 1], sgs, ALU.mult, ALU.mult,
                 wait=[(cACT.s, m_sg), frees.get("hfree")], inc=cDVE.inc())
            frees["tsc2"] = (cDVE.s, cDVE.n)
            frees["sgs"] = (cDVE.s, cDVE.n)
            frees[("hdone", ic)] = (cDVE.s, cDVE.n)
            if ic >= 2:
                jc = ic - 2
                S.op("pe", "matmul", bank(6, parts=1), ones_c,
                     hS[:, jc * 512:(jc + 1) * 512],
                     start=(jc == 0), stop=False, skip_group_check=True,
                     wait=[frees.get(("hdone", jc)),
                           frees.get("b6d") if jc == 0 else None])

        for jc in (NIC - 2, NIC - 1):
            S.op("pe", "matmul", bank(6, parts=1), ones_c,
                 hS[:, jc * 512:(jc + 1) * 512],
                 start=False, stop=(jc == NIC - 1), skip_group_check=True,
                 wait=[frees.get(("hdone", jc))],
                 inc=cPE.inc() if jc == NIC - 1 else None)
        m_hs = cPE.n
        S.op("dve", "tensor_scalar_mul", mean2, bank(6, parts=1), -128.0,
             wait=[(cPE.s, m_hs)])
        S.op("dve", "tensor_copy", shi2, mean2)
        S.op("dve", "tensor_sub", slo2, mean2, shi2, inc=cDVE.inc())
        m_hl2 = cDVE.n
        S.op("pe", "matmul", bank(7), ones_r, shi2, start=True, stop=False,
             wait=[(cDVE.s, m_hl2), frees.get("b7d")])
        S.op("pe", "matmul", bank(7), ones_r, slo2, start=False, stop=True,
             inc=cPE.inc())
        m_bc2 = cPE.n
        S.op("dve", "tensor_copy", xsb2, bank(7),
             wait=[(cPE.s, m_bc2), frees.get("xsb2")], inc=cDVE.inc())
        m_xsb2 = cDVE.n
        frees["b6d"] = (cDVE.s, m_xsb2)
        frees["b7d"] = (cDVE.s, m_xsb2)

        def d_stats(ce):
            dp = ce % 2
            S.op("pe", "matmul", bank(2, parts=1), ones_c, y2bfr[dp],
                 start=(ce == 0), stop=(ce == NEC - 1),
                 wait=[frees[("my2", ce)]] + ([frees.get("lnb2")] if ce == 0 else []))
            S.op("pe", "matmul", bank(3, parts=1), ones_c, sq2[dp],
                 start=(ce == 0), stop=(ce == NEC - 1),
                 inc=cPE.inc())
            frees[("y2bf", dp)] = (cPE.s, cPE.n)
            frees[("sq2", dp)] = (cPE.s, cPE.n)

        for ce in range(NEC):
            dp = ce % 2
            S.op("pool", "dma_start", out=dwp[dp], in_=dw_d[ce],
                 wait=[dwp_free.get(dp)], inc=dW.inc(16))
            d_dw = dW.n
            S.op("sync", "dma_start", out=x2r[dp],
                 in_=x2scr[ce * 128:(ce + 1) * 128, qc * 512:(qc + 1) * 512],
                 wait=[(dS.s, x2s_done), x2r_free.get(dp)], inc=dX.inc(16))
            d_dx = dX.n
            bp = ce % 2
            for ic in range(NIC):
                S.op("pe", "matmul", bank(bp),
                     dwp[dp][:, ic * 128:(ic + 1) * 128],
                     hS[:, ic * 512:(ic + 1) * 512],
                     start=(ic == 0), stop=(ic == NIC - 1),
                     wait=[(dW.s, d_dw), frees.get(("db", bp)),
                           frees.get(("dg", bp))] if ic == 0 else (),
                     inc=cPE.inc() if ic == NIC - 1 else None)
            m_dn = cPE.n
            dwp_free[dp] = (cPE.s, m_dn)
            yv = y2t(ce)
            S.op("dve", "tensor_add", tsc1, bank(bp), xsb2,
                 wait=[(cPE.s, m_dn), (cDVE.s, m_xsb2), frees.get("tsc1")])
            S.op("dve", "scalar_tensor_tensor", yv, tsc1, ds_s[:, ce:ce + 1],
                 x2r[dp], ALU.mult, ALU.add,
                 wait=[(dX.s, d_dx), frees.get(("y2", qc, ce))])
            frees[("db", bp)] = (cDVE.s, cDVE.n + 1)
            frees["tsc1"] = (cDVE.s, cDVE.n + 1)
            S.op("dve", "tensor_copy", y2bfr[dp], yv,
                 wait=[frees.get(("y2bf", dp))])
            S.op("dve", "tensor_mul", sq2[dp], y2bfr[dp], y2bfr[dp],
                 wait=[frees.get(("sq2", dp))], inc=cDVE.inc())
            m_y2 = cDVE.n
            frees[("my2", ce)] = (cDVE.s, m_y2)
            x2r_free[dp] = (cDVE.s, m_y2)
            if ce >= 1:
                d_stats(ce - 1)
        d_stats(NEC - 1)
        m_sums2 = cPE.n
        m_bcast2 = ln_stats_tail(2, 3, m_sums2,
                                 (mean2, var2, rstd2, mr2, shi2, slo2, mhi2, mlo2),
                                 (rb2, mrb2, 4, 5))
        frees["lnb2"] = (cDVE.s, m_bcast2)
        frees["xsb"] = (cDVE.s, m_bcast2)
        frees["xsb2"] = (cDVE.s, m_bcast2)
        for ce in range(NEC):
            op = ce % 2
            yv = y2t(ce)
            S.op("dve", "scalar_tensor_tensor", outs[op], yv, 1.0, rb2,
                 ALU.mult, ALU.mult,
                 wait=[(cDVE.s, m_bcast2), outs_free.get(op)])
            S.op("dve", "tensor_sub", outs[op], outs[op], mrb2,
                 inc=cDVE.inc())
            m_o = cDVE.n
            frees[("y2", qc + 1, ce)] = (cDVE.s, m_o)
            if ce % 4 == 3:
                gup_free[ce // 4] = (cDVE.s, m_o)
            S.op("sync", "dma_start",
                 out=out_d[ce * 128:(ce + 1) * 128, qc * 512:(qc + 1) * 512],
                 in_=outs[op], wait=[(cDVE.s, m_o)], inc=dS.inc(16))
            outs_free[op] = (dS.s, dS.n)
        frees["hfree"] = (cPE.s, cPE.n)
        frees[("dg", 2)] = (cDVE.s, cDVE.n)
        frees[("dg", 3)] = (cDVE.s, cDVE.n)

    S.raw("sync", lambda e, s=dS.s, v=dS.n: e.wait_ge(s, v))

    with nc.Block() as blk:
        blk.tensor(lambda e: [f(e) for f in S.q["pe"]])
        blk.scalar(lambda e: [f(e) for f in S.q["act"]])
        blk.vector(lambda e: [f(e) for f in S.q["dve"]])
        blk.gpsimd(lambda e: [f(e) for f in S.q["pool"]])
        blk.sync(lambda e: [f(e) for f in S.q["sync"]])
    return nc


def _rope_tables():
    half = DH // 2
    e = np.arange(half, dtype=np.float32) / np.float32(half)
    inv = (np.float32(1.0) / np.power(np.float32(10000.0), e)).astype(np.float32)
    ang = (np.arange(SEQ, dtype=np.float32)[:, None] * inv[None, :]).astype(np.float32)
    ang = np.concatenate([ang, ang], axis=1)
    return (np.ascontiguousarray(np.cos(ang).T.astype(np.float32)),
            np.ascontiguousarray(np.sin(ang).T.astype(np.float32)))


def _prep(inputs):
    x = np.asarray(inputs["L0_attn_hidden_states"], dtype=np.float32)[0]
    xt_full = np.ascontiguousarray(x.T)
    wq = np.asarray(inputs["L0_attn_wq"], dtype=np.float32)
    wk = np.asarray(inputs["L0_attn_wk"], dtype=np.float32)
    wv = np.asarray(inputs["L0_attn_wv"], dtype=np.float32)
    wo = np.asarray(inputs["L0_attn_wo"], dtype=np.float32)
    qsc = np.asarray(inputs["L0_attn_query_scale"], dtype=np.float32).reshape(128, 1)
    ksc = np.asarray(inputs["L0_attn_key_scale"], dtype=np.float32).reshape(128, 1)
    def u8(a):
        a = np.ascontiguousarray(np.asarray(a, dtype=np.int32))
        return np.ascontiguousarray(a.view(np.uint8).reshape(a.shape + (4,))[..., 0])
    gw = u8(inputs["L0_ffn_gate_w"])
    uw = u8(inputs["L0_ffn_up_w"])
    dw = u8(inputs["L0_ffn_down_w"])
    gsc = np.asarray(inputs["L0_ffn_gate_scale"], dtype=np.float32)
    usc = np.asarray(inputs["L0_ffn_up_scale"], dtype=np.float32)
    dsc = np.asarray(inputs["L0_ffn_down_scale"], dtype=np.float32)

    shared = {
        "wq_t": np.ascontiguousarray(wq.reshape(NEC, 128, NH, 128).transpose(2, 1, 0, 3).reshape(NH, 128, NEC * 128)),
        "wk_t": np.ascontiguousarray(wk.reshape(NEC, 128, NKV, 128).transpose(2, 1, 0, 3).reshape(NKV, 128, NEC * 128)),
        "wv_t": np.ascontiguousarray(wv.reshape(NEC, 128, 512)),
        "wo_t": np.ascontiguousarray(wo.reshape(NEC, 128, NEC, 128).transpose(2, 1, 0, 3).reshape(NEC, 128, NEC * 128)),
        "gw_t": np.ascontiguousarray(gw.reshape(NIC, 128, NEC, 128).transpose(0, 3, 2, 1).reshape(NIC, 128, NEC * 128)),
        "uw_t": np.ascontiguousarray(uw.reshape(NIC, 128, NEC, 128).transpose(0, 3, 2, 1).reshape(NIC, 128, NEC * 128)),
        "dw_t": np.ascontiguousarray(dw.reshape(NEC, 128, NIC, 128).transpose(0, 3, 2, 1).reshape(NEC, 128, NIC * 128)),
        "gs_t": np.ascontiguousarray(gsc.reshape(NIC, 128).T),
        "us_t": np.ascontiguousarray(usc.reshape(NIC, 128).T),
        "ds_t": np.ascontiguousarray(dsc.reshape(NEC, 128).T),
        "qsc": qsc, "ksc": ksc,
    }
    kk = np.arange(128)[:, None]
    qq = np.arange(128)[None, :]
    tpos = (kk <= qq).astype(ml_dtypes.bfloat16)
    tneg = (kk >= qq).astype(ml_dtypes.bfloat16)
    shared["tri"] = np.ascontiguousarray(np.concatenate([tpos, tneg], axis=1))
    rs = np.zeros((16, NH * 128), dtype=ml_dtypes.bfloat16)
    for h in range(NH):
        rs[h, h * 128:(h + 1) * 128] = 1
    shared["rsel"] = rs

    cosT, sinT = _rope_tables()
    maps = []
    for cid in range(NCORE):
        hs = cid * OWN - 1024
        zs = np.zeros((128, NKT * 31), dtype=ml_dtypes.bfloat16)
        valid = ((np.arange(hs, hs + HALO) >= 0) & (np.arange(hs, hs + HALO) < SEQ))
        for t in range(NKT):
            zs[:, t * 31 + 15] = valid[t * 128:(t + 1) * 128].astype(ml_dtypes.bfloat16)
        m = dict(shared)
        m["xt"] = np.ascontiguousarray(xt_full[:, cid * OWN:(cid + 1) * OWN])
        m["cos_t"] = np.ascontiguousarray(cosT[:, cid * OWN:(cid + 1) * OWN])
        m["sin_t"] = np.ascontiguousarray(sinT[:, cid * OWN:(cid + 1) * OWN])
        m["zsel"] = zs
        maps.append(m)
    return maps


_CACHE = {}


def _get_program(taps=()):
    key = tuple(sorted(taps))
    if key not in _CACHE:
        _CACHE[key] = build_program(taps)
    return _CACHE[key]


def kernel(**inputs):
    nc = _get_program()
    maps = _prep(inputs)
    res = run_bass_kernel_spmd(nc, maps, core_ids=list(range(NCORE)))
    out = np.concatenate([r["out_t"] for r in res.results], axis=1)
    return np.ascontiguousarray(out.T).reshape(1, SEQ, E)


# version 16
# speedup vs baseline: 1.0765x; 1.0765x over previous
import numpy as np
import ml_dtypes
import concourse.bass as bass
import concourse.mybir as mybir
from concourse.bass_utils import run_bass_kernel_spmd

F32 = mybir.dt.float32
BF16 = mybir.dt.bfloat16
U8 = mybir.dt.uint8
AF = mybir.ActivationFunctionType
ALU = mybir.AluOpType

NCORE = 8
SEQ = 8192
E = 2048
OWN = 1024
HALO = 3072
NH = 16
NKV = 4
DH = 128
NI = 8192
WIN = 1024
CAP = 50.0
NEC = E // 128
NIC = NI // 128
NKT = HALO // 128
EXPB = -25.0

BPP = 212863


def _al(n, a=64):
    return (n + a - 1) // a * a


def _blocks(qc):
    Q0 = 1024 + qc * 512
    out = []
    for o in range(-WIN, WIN + 512 - 128 + 1, 128):
        js = [j for j in range(4) if abs(o - 128 * j) <= WIN]
        if not js:
            continue
        tris = [(j, +1) for j in js if o - 128 * j == WIN] + \
               [(j, -1) for j in js if o - 128 * j == -WIN]
        out.append((Q0 + o, js[0], js[-1], tris))
    assert len(out) == 20
    return out


class Ctr:
    def __init__(self, nc, name):
        self.s = nc.alloc_semaphore(name)
        self.n = 0

    def inc(self, k=1):
        self.n += k
        return (self.s, k)


class Sch:
    ENGS = ("pe", "act", "dve", "pool", "sync")

    def __init__(self, nc):
        self.nc = nc
        self.q = {e: [] for e in self.ENGS}
        self.hw = {e: {} for e in self.ENGS}

    def op(self, eng, meth, *args, wait=(), inc=None, **kw):
        real_waits = []
        for w in wait:
            if w is None:
                continue
            s, v = w
            if self.hw[eng].get(s.name, -1) < v:
                self.hw[eng][s.name] = v
                real_waits.append((s, v))
        if inc is not None:
            s, _ = inc
        def f(e, m=meth, a=args, k=kw, rw=tuple(real_waits), inc=inc):
            for (s, v) in rw:
                e.wait_ge(s, v)
            ins = getattr(e, m)(*a, **k)
            if inc is not None:
                ins.then_inc(inc[0], inc[1])
        self.q[eng].append(f)

    def raw(self, eng, fn):
        self.q[eng].append(fn)

    def barrier(self, ctrs):
        vals = {}
        for e in self.ENGS:
            c = ctrs[e]
            tgt = c.inc(1)
            vals[e] = (c.s, c.n)
            self.raw(e, lambda eng, s=tgt[0]: eng.drain().then_inc(s, 1))
        for e in self.ENGS:
            for o in self.ENGS:
                if o == e:
                    continue
                s, v = vals[o]
                if self.hw[e].get(s.name, -1) < v:
                    self.hw[e][s.name] = v
                    self.raw(e, lambda eng, s=s, v=v: eng.wait_ge(s, v))


def build_program(taps=()):
    nc = bass.Bass("TRN2", target_bir_lowering=False, debug=False,
                   detect_race_conditions=False)

    def din(name, shape, dt):
        return nc.dram_tensor(name, list(shape), dt, kind="ExternalInput").ap()

    xt = din("xt", (E, HALO), F32)
    cos_t = din("cos_t", (DH, HALO), F32)
    sin_t = din("sin_t", (DH, HALO), F32)
    zsel_d = din("zsel", (128, NKT * 31), BF16)
    tri_d = din("tri", (128, 256), BF16)
    rsel_d = din("rsel", (16, NH * 128), BF16)
    wq_d = din("wq_t", (NH, 128, NEC * 128), F32)
    wk_d = din("wk_t", (NKV, 128, NEC * 128), F32)
    wv_d = din("wv_t", (NEC, 128, 512), F32)
    wo_d = din("wo_t", (NEC, 128, NEC * 128), F32)
    gw_d = din("gw_t", (NIC, 128, NEC * 128), U8)
    uw_d = din("uw_t", (NIC, 128, NEC * 128), U8)
    dw_d = din("dw_t", (NEC, 128, NIC * 128), U8)
    gs_d = din("gs_t", (128, NIC), F32)
    us_d = din("us_t", (128, NIC), F32)
    ds_d = din("ds_t", (128, NEC), F32)
    qsc_d = din("qsc", (128, 1), F32)
    ksc_d = din("ksc", (128, 1), F32)

    out_d = nc.dram_tensor("out_t", [E, OWN], F32, kind="ExternalOutput").ap()
    x2k = "ExternalOutput" if "x2" in taps else "Internal"
    x2scr = nc.dram_tensor("x2scr", [E, OWN], F32, kind=x2k).ap()
    dbg = {}
    if "qkv" in taps:
        dbg["k"] = nc.dram_tensor("dbg_k", [128, NKV * HALO], BF16, kind="ExternalOutput").ap()
        dbg["v"] = nc.dram_tensor("dbg_v", [128, NKT * 512], BF16, kind="ExternalOutput").ap()
        dbg["q"] = nc.dram_tensor("dbg_q", [128, NH * OWN], BF16, kind="ExternalOutput").ap()
    if "attn" in taps:
        dbg["attn"] = nc.dram_tensor("dbg_attn", [128, NH * OWN], BF16, kind="ExternalOutput").ap()

    ARENA_B = (BPP - 256) // 64 * 64
    AR = nc.alloc_sbuf_tensor("arena", [128, ARENA_B], U8).ap()

    def view(off, parts, cols, dt):
        sz = {F32: 4, BF16: 2, U8: 1}[dt]
        nb = cols * sz
        assert off % 64 == 0 and off + nb <= ARENA_B, (off, nb)
        return AR[0:parts, off:off + nb].bitcast(dt)

    o = 0
    zsel = view(o, 128, NKT * 31, BF16); o = _al(o + NKT * 31 * 2)
    tri = view(o, 128, 256, BF16); o = _al(o + 512)
    rsel = view(o, 16, NH * 128, BF16); o = _al(o + NH * 128 * 2)
    gs_s = view(o, 128, NIC, F32); o = _al(o + NIC * 4)
    us_s = view(o, 128, NIC, F32); o = _al(o + NIC * 4)
    ds_s = view(o, 128, NEC, F32); o = _al(o + NEC * 4)
    qsc_s = view(o, 128, 1, F32); o = _al(o + 4)
    ksc_s = view(o, 128, 1, F32); o = _al(o + 4)
    neg25 = view(o, 128, 1, F32); o = _al(o + 4)
    eps12 = view(o, 128, 1, F32); o = _al(o + 4)
    eps5 = view(o, 128, 1, F32); o = _al(o + 4)
    ones_c = view(o, 128, 1, BF16); o = _al(o + 2)
    ones_r = view(o, 1, 128, BF16); o = _al(o + 256)
    P_END = o
    assert P_END < 8192

    a = P_END
    def Av(parts, cols, dt):
        nonlocal a
        v = view(a, parts, cols, dt)
        a = _al(a + cols * {F32: 4, BF16: 2, U8: 1}[dt])
        return v
    x_own = Av(128, NEC * OWN, BF16)
    xb = [Av(128, NEC * 512, BF16) for _ in range(2)]
    wpA = [Av(128, NEC * 128, BF16) for _ in range(2)]
    wvp = Av(128, NEC * 512, BF16)
    cosb = [Av(128, 512, F32) for _ in range(2)]
    sinb = [Av(128, 512, F32) for _ in range(2)]
    TOPS = 169984 + NH * OWN * 2
    rot_o = [a, TOPS]
    rot = [Av(128, 512, F32), view(TOPS, 128, 512, F32)]
    qa = [Av(128, 512, F32), view(TOPS + 2048, 128, 512, F32)]
    qr = [Av(128, 512, F32), view(TOPS + 4096, 128, 512, F32),
          view(TOPS + 6144, 128, 512, F32)]
    sqb = [view(rot_o[i], 128, 512, BF16) for i in range(2)]
    rnrow = [Av(1, 512, F32) for _ in range(2)]
    rnhi = [Av(1, 512, BF16) for _ in range(2)]
    rnlo = [Av(1, 512, BF16) for _ in range(2)]
    assert a <= 120832, a

    KO, VO, QO = 120832, 145408, 169984
    K_fm = view(KO, 128, NKV * HALO, BF16)
    V_tm = view(VO, 128, NKT * 512, BF16)
    Q_fm = view(QO, 128, NH * OWN, BF16)
    wkp = view(QO, 128, NKV * NEC * 128, BF16)
    assert QO + NH * OWN * 2 <= ARENA_B

    b = P_END
    def Bv(parts, cols, dt):
        nonlocal b
        v = view(b, parts, cols, dt)
        b = _al(b + cols * {F32: 4, BF16: 2, U8: 1}[dt])
        return v
    pt = [Bv(128, 1024, BF16) for _ in range(3)]
    th = [Bv(128, 1024, F32) for _ in range(3)]
    rzf = Bv(16, 512, F32)
    rzhi = Bv(16, 512, BF16)
    rzlo = Bv(16, 512, BF16)
    assert b <= 84992, b
    attn = view(84992, 128, NH * OWN, BF16)

    x2bf = view(P_END, 128, NEC * OWN, BF16)
    c = _al(P_END + NEC * OWN * 2)
    def Cv(parts, cols, dt):
        nonlocal c
        v = view(c, parts, cols, dt)
        c = _al(c + cols * {F32: 4, BF16: 2, U8: 1}[dt])
        return v
    wpC = [Cv(128, NEC * 128, BF16) for _ in range(2)]
    xres = [Cv(128, 512, F32) for _ in range(4)]
    rbsb = Cv(128, 512, F32)
    mrbsb = Cv(128, 512, F32)
    sqc = [Cv(128, 512, BF16) for _ in range(4)]
    ybfr = [Cv(128, 512, BF16) for _ in range(4)]
    meanr = Cv(1, 512, F32)
    varr = Cv(1, 512, F32)
    rstdr = Cv(1, 512, F32)
    mrr = Cv(1, 512, F32)
    shi = Cv(1, 512, BF16)
    slo = Cv(1, 512, BF16)
    mhi = Cv(1, 512, BF16)
    mlo = Cv(1, 512, BF16)
    assert c <= 84992, c
    yC = view(120832, 128, NEC * 512, F32)

    d = _al(P_END + NEC * OWN * 2)
    def Dv(parts, cols, dt):
        nonlocal d
        v = view(d, parts, cols, dt)
        d = _al(d + cols * {F32: 4, BF16: 2, U8: 1}[dt])
        return v
    dwp = [Dv(128, NIC * 128, BF16) for _ in range(2)]
    outs = [Dv(128, 512, F32) for _ in range(2)]
    x2r = [Dv(128, 512, F32) for _ in range(2)]
    xsb = [Dv(128, 512, F32) for _ in range(2)]
    xsb2 = Dv(128, 512, F32)
    tsc1 = Dv(128, 512, F32)
    tsc2 = Dv(128, 512, F32)
    sgs = Dv(128, 512, F32)
    y2bfr = [Dv(128, 512, BF16) for _ in range(2)]
    mean2 = Dv(1, 512, F32)
    var2 = Dv(1, 512, F32)
    rstd2 = Dv(1, 512, F32)
    mr2 = Dv(1, 512, F32)
    shi2 = Dv(1, 512, BF16)
    slo2 = Dv(1, 512, BF16)
    mhi2 = Dv(1, 512, BF16)
    mlo2 = Dv(1, 512, BF16)
    rb2 = Dv(128, 512, F32)
    mrb2 = Dv(128, 512, F32)
    sq2 = [Dv(128, 512, BF16) for _ in range(2)]
    GUP_O = _al(d)
    assert GUP_O + 32 * 1024 + NIC * 512 * 2 <= ARENA_B, GUP_O
    gup = [view(GUP_O + i * 8192, 128, 2 * NEC * 128, BF16) for i in range(4)]
    y2a = view(GUP_O, 128, 8 * 512, F32)
    y2b = view(GUP_O + 16384, 128, 8 * 512, F32)
    HS_O = GUP_O + 32768
    hS = view(HS_O, 128, NIC * 512, BF16)

    def y2t(ce):
        return (y2a if ce < 8 else y2b)[:, (ce % 8) * 512:(ce % 8) * 512 + 512]

    PS = nc.alloc_psum_tensor("PS", [128, 4096], F32).ap()

    def bank(i, n=1, parts=128):
        return PS[0:parts, i * 512:(i + n) * 512]

    S = Sch(nc)
    dW = Ctr(nc, "dW")
    dX = Ctr(nc, "dX")
    dS = Ctr(nc, "dS")
    cPE = Ctr(nc, "cPE")
    cACT = Ctr(nc, "cACT")
    cDVE = Ctr(nc, "cDVE")
    cPOOL = Ctr(nc, "cPOOL")
    cSYNC = Ctr(nc, "cSYNC")
    CTRS = {"pe": cPE, "act": cACT, "dve": cDVE, "pool": cPOOL, "sync": cSYNC}

    S.op("sync", "dma_start", out=zsel, in_=zsel_d, inc=dX.inc(16))
    S.op("sync", "dma_start", out=tri, in_=tri_d, inc=dX.inc(16))
    S.op("sync", "dma_start", out=rsel, in_=rsel_d, inc=dX.inc(16))
    S.op("sync", "dma_start", out=gs_s, in_=gs_d, inc=dX.inc(16))
    S.op("sync", "dma_start", out=us_s, in_=us_d, inc=dX.inc(16))
    S.op("sync", "dma_start", out=ds_s, in_=ds_d, inc=dX.inc(16))
    S.op("sync", "dma_start", out=qsc_s, in_=qsc_d, inc=dX.inc(16))
    S.op("sync", "dma_start", out=ksc_s, in_=ksc_d, inc=dX.inc(16))
    preload_dx = dX.n
    S.op("dve", "memset", neg25, EXPB)
    S.op("dve", "memset", eps12, 1e-12)
    S.op("dve", "memset", eps5, 1e-5)
    S.op("dve", "memset", ones_c, 1.0)
    S.op("dve", "memset", ones_r, 1.0, inc=cDVE.inc())
    m_pre_dve = cDVE.n

    def xv(cin, T):
        if T in (2, 3):
            return x_own[:, cin * OWN + (T - 2) * 512: cin * OWN + (T - 2) * 512 + 512]
        return xb[T % 2][:, cin * 512: cin * 512 + 512]

    frees = {}

    chain_i = [0]

    def chain_front(ps_pp, cosv, sinv, m_proj):
        ci = chain_i[0]
        chain_i[0] += 1
        s = ci % 2
        s3 = ci % 3
        ps = bank(ps_pp)
        S.op("act", "activation", rot[s][0:64, :], ps[64:128, :], AF.Copy, scale=-1.0,
             wait=[(cPE.s, m_proj), frees.get(("rot", s)), frees.get(("sqb", s))])
        S.op("act", "activation", rot[s][64:128, :], ps[0:64, :], AF.Copy,
             inc=cACT.inc())
        m_rot = cACT.n
        S.op("dve", "tensor_mul", qa[s], ps, cosv, wait=[(cACT.s, m_rot)])
        S.op("dve", "tensor_mul", rot[s], rot[s], sinv)
        S.op("dve", "tensor_add", qr[s3], qa[s], rot[s],
             wait=[frees.get(("qr", s3))])
        S.op("dve", "tensor_mul", sqb[s], qr[s3], qr[s3],
             inc=cDVE.inc())
        m_qr = cDVE.n
        frees[("rot", s)] = (cDVE.s, m_qr)
        return dict(s=s, s3=s3, r=s % 2, m_qr=m_qr)

    def chain_tail1(ctx):
        s, r, m_qr = ctx["s"], ctx["r"], ctx["m_qr"]
        S.op("pe", "matmul", bank(3, parts=1), ones_c, sqb[s], start=True, stop=True,
             wait=[(cDVE.s, m_qr), frees.get("b3")], inc=cPE.inc())
        m_ss = cPE.n
        frees[("sqb", s)] = (cPE.s, m_ss)
        S.op("act", "activation", rnrow[r], bank(3, parts=1), AF.Ln,
             bias=eps12[0:1, :],
             wait=[(cPE.s, m_ss), frees.get(("rnrow", r))])
        S.op("act", "activation", rnrow[r], rnrow[r], AF.Exp, scale=-0.5,
             inc=cACT.inc())
        m_rn = cACT.n
        frees["b3"] = (cACT.s, m_rn)
        S.op("dve", "tensor_copy", rnhi[r], rnrow[r],
             wait=[(cACT.s, m_rn), frees.get(("rows", r))])
        S.op("dve", "tensor_sub", rnlo[r], rnrow[r], rnhi[r], inc=cDVE.inc())
        m_rows = cDVE.n
        frees[("rnrow", r)] = (cDVE.s, m_rows)
        ctx["m_rows"] = m_rows

    def chain_tail2(ctx, sc_ap, dest):
        s3, r, m_rows = ctx["s3"], ctx["r"], ctx["m_rows"]
        S.op("pe", "matmul", bank(2), ones_r, rnhi[r], start=True, stop=False,
             wait=[(cDVE.s, m_rows), frees.get("b2")])
        S.op("pe", "matmul", bank(2), ones_r, rnlo[r], start=False, stop=True,
             inc=cPE.inc())
        m_r1 = cPE.n
        frees[("rows", r)] = (cPE.s, m_r1)
        S.op("dve", "scalar_tensor_tensor", dest, qr[s3], sc_ap, bank(2),
             ALU.mult, ALU.mult, wait=[(cPE.s, m_r1)], inc=cDVE.inc())
        m_fin = cDVE.n
        frees[("qr", s3)] = (cDVE.s, m_fin)
        frees["b2"] = (cDVE.s, m_fin)
        return m_fin


    nproj = [0]
    projfree = {}
    PROJ_BANKS = (0, 1, 6, 7)

    def do_proj(lhs_tiles, rhs_tiles, extra_wait=()):
        pp = PROJ_BANKS[nproj[0] % 4]
        nproj[0] += 1
        w = [projfree.get(pp)] + list(extra_wait)
        for ci in range(NEC):
            S.op("pe", "matmul", bank(pp), lhs_tiles(ci), rhs_tiles(ci),
                 start=(ci == 0), stop=(ci == NEC - 1),
                 wait=w if ci == 0 else (),
                 inc=cPE.inc() if ci == NEC - 1 else None)
        return pp, cPE.n

    xt_pc = xt.rearrange("(c p) t -> p c t", p=128)
    xb_free = {}
    pend = []

    def _advance(new):
        pend.append(new)
        if len(pend) >= 2:
            c = pend[-2]
            if "m_rows" not in c[0]:
                chain_tail1(c[0])
        if len(pend) >= 3:
            c = pend.pop(0)
            chain_tail2(c[0], c[1], c[2])

    def _flush_all():
        while pend:
            c = pend.pop(0)
            if "m_rows" not in c[0]:
                chain_tail1(c[0])
            chain_tail2(c[0], c[1], c[2])

    wk_dw = [0]
    wv_dw = [0]
    for T in range(6):
        dst_is_own = T in (2, 3)
        if dst_is_own:
            dst3 = x_own.rearrange("p (c t) -> p c t", c=NEC)[
                :, :, (T - 2) * 512:(T - 2) * 512 + 512]
        else:
            dst3 = xb[T % 2].rearrange("p (c t) -> p c t", c=NEC)
        S.op("pool", "dma_start", out=dst3,
             in_=xt_pc[:, :, T * 512:(T + 1) * 512],
             wait=[xb_free.get(T % 2)] if not dst_is_own else (),
             inc=dW.inc(16))
        x_dw = dW.n
        if T == 0:
            wk_g_dw = []
            for g in range(NKV):
                S.op("pool", "dma_start",
                     out=wkp[:, g * 2048:(g + 1) * 2048],
                     in_=wk_d[g], inc=dW.inc(16))
                wk_g_dw.append(dW.n)
            wk_dw[0] = wk_g_dw
            S.op("pool", "dma_start", out=wvp.rearrange("p (g c) -> p g c", g=NEC),
                 in_=wv_d.transpose([1, 0, 2]), inc=dW.inc(16))
            wv_dw[0] = dW.n
        S.op("sync", "dma_start", out=cosb[T % 2], in_=cos_t[:, T * 512:(T + 1) * 512],
             wait=[frees.get(("cos", T % 2))], inc=dX.inc(16))
        S.op("sync", "dma_start", out=sinb[T % 2], in_=sin_t[:, T * 512:(T + 1) * 512],
             inc=dX.inc(16))
        cs_dx = dX.n
        for g in range(NKV):
            pp, m_proj = do_proj(
                lambda ci, g=g: wkp[:, g * 2048 + ci * 128: g * 2048 + ci * 128 + 128],
                lambda ci, T=T: xv(ci, T),
                extra_wait=[(dW.s, max(x_dw, wk_dw[0][g])), (dX.s, cs_dx),
                            (dX.s, preload_dx), (cDVE.s, m_pre_dve)])
            ctx = chain_front(pp, cosb[T % 2], sinb[T % 2], m_proj)
            projfree[pp] = (cDVE.s, ctx["m_qr"])
            frees[("cos", T % 2)] = (cDVE.s, ctx["m_qr"])
            _advance((ctx, ksc_s,
                      K_fm[:, g * HALO + T * 512: g * HALO + T * 512 + 512]))
        for sub in range(4):
            pp = 4 + (sub % 2)
            for ci in range(NEC):
                S.op("pe", "matmul", bank(pp),
                     xv(ci, T)[:, sub * 128:sub * 128 + 128],
                     wvp[:, ci * 512:(ci + 1) * 512],
                     start=(ci == 0), stop=(ci == NEC - 1),
                     wait=[frees.get(("bv", pp)), (dW.s, wv_dw[0])] if ci == 0 else (),
                     inc=cPE.inc() if ci == NEC - 1 else None)
            m_v = cPE.n
            kt = T * 4 + sub
            S.op("dve", "tensor_copy", V_tm[:, kt * 512:(kt + 1) * 512], bank(pp),
                 wait=[(cPE.s, m_v)], inc=cDVE.inc())
            frees[("bv", pp)] = (cDVE.s, cDVE.n)
        xb_free[T % 2] = (cPE.s, cPE.n)

    for i, T in enumerate((2, 3)):
        S.op("sync", "dma_start", out=cosb[i], in_=cos_t[:, T * 512:(T + 1) * 512],
             wait=[frees.get(("cos", i))], inc=dX.inc(16))
        S.op("sync", "dma_start", out=sinb[i], in_=sin_t[:, T * 512:(T + 1) * 512],
             inc=dX.inc(16))
    q_cs_dx = dX.n
    wp_free = {}
    for h in range(NH):
        hp = h % 2
        S.op("pool", "dma_start", out=wpA[hp], in_=wq_d[h],
             wait=[wp_free.get(hp)], inc=dW.inc(16))
        wq_dw = dW.n
        for t2 in range(2):
            pp, m_proj = do_proj(
                lambda ci, hp=hp: wpA[hp][:, ci * 128:(ci + 1) * 128],
                lambda ci, t2=t2: x_own[:, ci * OWN + t2 * 512: ci * OWN + t2 * 512 + 512],
                extra_wait=[(dW.s, wq_dw), (dX.s, q_cs_dx)])
            ctx = chain_front(pp, cosb[t2], sinb[t2], m_proj)
            projfree[pp] = (cDVE.s, ctx["m_qr"])
            _advance((ctx, qsc_s,
                      Q_fm[:, h * OWN + t2 * 512: h * OWN + t2 * 512 + 512]))
        wp_free[hp] = (cPE.s, cPE.n)
    _flush_all()

    S.barrier(CTRS)

    if "qkv" in taps:
        S.op("sync", "dma_start", out=dbg["k"], in_=K_fm, inc=dS.inc(16))
        S.op("sync", "dma_start", out=dbg["v"], in_=V_tm, inc=dS.inc(16))
        S.op("sync", "dma_start", out=dbg["q"], in_=Q_fm, inc=dS.inc(16))

    st_free = {}
    pt_free = {}
    th_free = {}
    pv_free = {}
    nrm_free = None
    z_read = None
    post = []

    for qc in range(2):
        blocks = _blocks(qc)
        groups = [blocks[i:i + 2] for i in range(0, 20, 2)]
        z_started = [False]
        nz_total = NH * 20
        nz = [0]
        flat = [(h, gi, grp) for h in range(NH) for gi, grp in enumerate(groups)]
        pv_cnt = {h: 0 for h in range(NH)}
        ginfo = {}

        def emit_pvz(idx, flat=flat, pv_cnt=pv_cnt, ginfo=ginfo,
                     z_started=z_started, nz=nz, nz_total=nz_total, qc=qc):
            h, gi, grp = flat[idx]
            g = h // 4
            hv = 4 + h % 2
            p, m_exp, m_tri = ginfo.pop(idx)
            for bi, (K0, jlo, jhi, tris) in enumerate(grp):
                kt = K0 // 128
                n0, n1 = jlo * 128, (jhi + 1) * 128
                rhs = pt[p][:, bi * 512 + n0: bi * 512 + n1]
                w = [(cACT.s, m_exp)]
                if tris:
                    w.append((cDVE.s, m_tri))
                if pv_cnt[h] == 0:
                    w.append(pv_free.get(hv))
                S.op("pe", "matmul", bank(hv)[:, n0:n1],
                     V_tm[:, kt * 512 + g * 128: kt * 512 + g * 128 + 128],
                     rhs, start=(pv_cnt[h] == 0), stop=(pv_cnt[h] == 19),
                     skip_group_check=True, wait=w)
                pv_cnt[h] += 1
                zw = []
                if not z_started[0]:
                    zw = [z_read] if z_read else []
                nz[0] += 1
                S.op("pe", "matmul", bank(7, parts=16)[:, n0:n1],
                     zsel[:, kt * 31 + (15 - h): kt * 31 + (15 - h) + 16],
                     rhs, start=(not z_started[0]), stop=(nz[0] == nz_total),
                     skip_group_check=True, wait=zw,
                     inc=cPE.inc())
                z_started[0] = True
                pt_free[p] = (cPE.s, cPE.n)
            if pv_cnt[h] == 20:
                S.op("dve", "tensor_copy",
                     attn[:, h * OWN + qc * 512: h * OWN + qc * 512 + 512],
                     bank(hv), wait=[(cPE.s, cPE.n)], inc=cDVE.inc())
                pv_free[hv] = (cDVE.s, cDVE.n)

        for idx, (h, gi, grp) in enumerate(flat):
            g = h // 4
            p = idx % 2
            p3 = idx % 3
            gsz = len(grp)
            a0 = grp[0][1] * 128
            a1 = (gsz - 1) * 512 + (grp[-1][2] + 1) * 128
            for bi, (K0, jlo, jhi, tris) in enumerate(grp):
                n0, n1 = jlo * 128, (jhi + 1) * 128
                w = [st_free.get(p)] if bi == 0 else ()
                S.op("pe", "matmul",
                     PS[:, (2 * p + bi) * 512 + n0: (2 * p + bi) * 512 + n1],
                     K_fm[:, g * HALO + K0: g * HALO + K0 + 128],
                     Q_fm[:, h * OWN + qc * 512 + n0: h * OWN + qc * 512 + n1],
                     start=True, stop=True, wait=w,
                     inc=cPE.inc() if bi == gsz - 1 else None)
            m_sc = cPE.n
            S.op("act", "activation", th[p3][:, a0:a1],
                 PS[:, 2 * p * 512 + a0: 2 * p * 512 + a1], AF.Tanh,
                 scale=1.0 / (np.sqrt(DH) * CAP),
                 wait=[(cPE.s, m_sc), th_free.get(p3)], inc=cACT.inc())
            st_free[p] = (cACT.s, cACT.n)
            S.op("act", "activation", pt[p3][:, a0:a1], th[p3][:, a0:a1],
                 AF.Exp, scale=CAP, bias=neg25,
                 wait=[pt_free.get(p3)], inc=cACT.inc())
            m_exp = cACT.n
            th_free[p3] = (cACT.s, m_exp)
            m_tri = cDVE.n
            ntri = sum(len(t[3]) for t in grp)
            if ntri:
                cnt = 0
                for bi, (K0, jlo, jhi, tris) in enumerate(grp):
                    for (j, sign) in tris:
                        cnt += 1
                        sl = pt[p3][:, bi * 512 + j * 128: bi * 512 + j * 128 + 128]
                        tm = tri[:, 0:128] if sign > 0 else tri[:, 128:256]
                        S.op("dve", "tensor_mul", sl, sl, tm,
                             wait=[(cACT.s, m_exp)],
                             inc=cDVE.inc() if cnt == ntri else None)
                m_tri = cDVE.n
            ginfo[idx] = (p3, m_exp, m_tri)
            if idx >= 2:
                emit_pvz(idx - 2)
            if post:
                post.pop(0)()
        emit_pvz(len(flat) - 2)
        emit_pvz(len(flat) - 1)
        while post:
            post.pop(0)()
        m_z = cPE.n
        S.op("dve", "reciprocal", rzf, bank(7, parts=16), wait=[(cPE.s, m_z)])
        S.op("dve", "tensor_copy", rzhi, rzf)
        S.op("dve", "tensor_sub", rzlo, rzf, rzhi, inc=cDVE.inc())
        m_rz = cDVE.n
        z_read = (cDVE.s, m_rz)

        def mk_norm(h, qc=qc, m_rz=m_rz):
            def go():
                nonlocal nrm_free
                S.op("pe", "matmul", bank(6), rsel[:, h * 128:(h + 1) * 128],
                     rzhi, start=True, stop=False,
                     wait=[(cDVE.s, m_rz), nrm_free,
                           pv_free.get(4 + h % 2)])
                S.op("pe", "matmul", bank(6), rsel[:, h * 128:(h + 1) * 128],
                     rzlo, start=False, stop=True, inc=cPE.inc())
                m_r1 = cPE.n
                sl = attn[:, h * OWN + qc * 512: h * OWN + qc * 512 + 512]
                S.op("dve", "tensor_mul", sl, sl, bank(6),
                     wait=[(cPE.s, m_r1)], inc=cDVE.inc())
                nrm_free = (cDVE.s, cDVE.n)
            return go
        for h in range(NH):
            post.append(mk_norm(h))
    while post:
        post.pop(0)()

    S.barrier(CTRS)
    if "attn" in taps:
        S.op("sync", "dma_start", out=dbg["attn"], in_=attn, inc=dS.inc(16))

    def ln_stats_tail(sbank, qbank, m_sums, rows, bvec):
        meanv, varv, rstdv, mrv, hiv, lov, mhiv, mlov = rows
        rbv, mrbv, bb1, bb2 = bvec
        S.op("dve", "tensor_scalar_mul", meanv, bank(sbank, parts=1), 1.0 / E,
             wait=[(cPE.s, m_sums)])
        S.op("dve", "tensor_mul", varv, meanv, meanv)
        S.op("dve", "scalar_tensor_tensor", varv, bank(qbank, parts=1), 1.0 / E,
             varv, ALU.mult, ALU.subtract, inc=cDVE.inc())
        m_var = cDVE.n
        S.op("act", "activation", rstdv, varv, AF.Ln, bias=eps5[0:1, :],
             wait=[(cDVE.s, m_var)])
        S.op("act", "activation", rstdv, rstdv, AF.Exp, scale=-0.5,
             inc=cACT.inc())
        m_rstd = cACT.n
        S.op("dve", "tensor_mul", mrv, meanv, rstdv, wait=[(cACT.s, m_rstd)])
        S.op("dve", "tensor_copy", hiv, rstdv)
        S.op("dve", "tensor_sub", lov, rstdv, hiv)
        S.op("dve", "tensor_copy", mhiv, mrv)
        S.op("dve", "tensor_sub", mlov, mrv, mhiv, inc=cDVE.inc())
        m_rows2 = cDVE.n
        S.op("pe", "matmul", bank(bb1), ones_r, hiv, start=True, stop=False,
             wait=[(cDVE.s, m_rows2)])
        S.op("pe", "matmul", bank(bb1), ones_r, lov, start=False, stop=True)
        S.op("pe", "matmul", bank(bb2), ones_r, mhiv, start=True, stop=False)
        S.op("pe", "matmul", bank(bb2), ones_r, mlov, start=False, stop=True,
             inc=cPE.inc())
        m_bc = cPE.n
        S.op("dve", "tensor_copy", rbv, bank(bb1), wait=[(cPE.s, m_bc)])
        S.op("dve", "tensor_copy", mrbv, bank(bb2), inc=cDVE.inc())
        return cDVE.n

    wpc_free = {}
    xres_free = {}
    m_x2bf = {}
    gup_free = {}
    dwp_free = {}
    yB = [yC, view(153600, 128, NEC * 512, F32)]

    def c_stats(ce, qc):
        m_y = frees[("my", ce, qc)]
        S.op("pe", "matmul", bank(2 + 2 * qc, parts=1), ones_c,
             ybfr[2 * qc + ce % 2],
             start=(ce == 0), stop=(ce == NEC - 1),
             wait=[m_y])
        S.op("pe", "matmul", bank(3 + 2 * qc, parts=1), ones_c,
             sqc[2 * qc + ce % 2],
             start=(ce == 0), stop=(ce == NEC - 1),
             inc=cPE.inc())
        frees[("ybf", qc, ce % 2)] = (cPE.s, cPE.n)
        frees[("sqc", qc, ce % 2)] = (cPE.s, cPE.n)

    for ce in range(NEC):
        cp = ce % 2
        S.op("pool", "dma_start", out=wpC[cp], in_=wo_d[ce],
             wait=[wpc_free.get(cp)], inc=dW.inc(16))
        wo_dw = dW.n
        for qc in range(2):
            xr = xres[2 * qc + ce % 2]
            S.op("sync", "dma_start", out=xr,
                 in_=xt[ce * 128:(ce + 1) * 128, 1024 + qc * 512: 1536 + qc * 512],
                 wait=[xres_free.get((qc, ce % 2))], inc=dX.inc(16))
            xr_dx = dX.n
            for ch in range(NEC):
                S.op("pe", "matmul", bank(qc),
                     wpC[cp][:, ch * 128:(ch + 1) * 128],
                     attn[:, ch * OWN + qc * 512: ch * OWN + qc * 512 + 512],
                     start=(ch == 0), stop=(ch == NEC - 1),
                     wait=[(dW.s, wo_dw), frees.get(("cb", qc))] if ch == 0 else (),
                     inc=cPE.inc() if ch == NEC - 1 else None)
            m_wo = cPE.n
            yv = yB[qc][:, ce * 512:(ce + 1) * 512]
            S.op("dve", "tensor_add", yv, bank(qc), xr,
                 wait=[(cPE.s, m_wo), (dX.s, xr_dx)])
            frees[("cb", qc)] = (cDVE.s, cDVE.n + 1)
            S.op("dve", "tensor_copy", ybfr[2 * qc + ce % 2], yv,
                 wait=[frees.get(("ybf", qc, ce % 2))])
            S.op("dve", "tensor_mul", sqc[2 * qc + ce % 2], ybfr[2 * qc + ce % 2],
                 ybfr[2 * qc + ce % 2],
                 wait=[frees.get(("sqc", qc, ce % 2))], inc=cDVE.inc())
            frees[("my", ce, qc)] = (cDVE.s, cDVE.n)
            xres_free[(qc, ce % 2)] = (cDVE.s, cDVE.n)
            if ce >= 1:
                c_stats(ce - 1, qc)
        wpc_free[cp] = (cPE.s, cPE.n)
    for qc in range(2):
        c_stats(NEC - 1, qc)
    m_sums = cPE.n
    x2s_chunk = {}
    for qc in range(2):
        bb = (6, 7) if qc == 0 else (4, 5)
        m_bcast = ln_stats_tail(2 + 2 * qc, 3 + 2 * qc, m_sums,
                                (meanr, varr, rstdr, mrr, shi, slo, mhi, mlo),
                                (rbsb, mrbsb, bb[0], bb[1]))
        if qc == 0:
            frees["b6d"] = (cDVE.s, m_bcast)
            frees["b7d"] = (cDVE.s, m_bcast)
        for ce in range(NEC):
            yv = yB[qc][:, ce * 512:(ce + 1) * 512]
            S.op("dve", "scalar_tensor_tensor", yv, yv, 1.0, rbsb,
                 ALU.mult, ALU.mult, wait=[(cDVE.s, m_bcast)])
            S.op("dve", "tensor_sub", yv, yv, mrbsb)
            S.op("dve", "tensor_copy",
                 x2bf[:, ce * OWN + qc * 512: ce * OWN + qc * 512 + 512], yv,
                 inc=cDVE.inc())
            m_x2 = cDVE.n
            frees[("x2t", qc, ce)] = (cDVE.s, m_x2)
            S.op("sync", "dma_start",
                 out=x2scr[ce * 128:(ce + 1) * 128, qc * 512:(qc + 1) * 512],
                 in_=yv, wait=[(cDVE.s, m_x2)], inc=dS.inc(16))
            frees[("y0s", qc, ce)] = (dS.s, dS.n)
        m_x2bf[qc] = cDVE.n
        x2s_chunk[qc] = dS.n
    x2s_done = dS.n

    frees[("dg", 0)] = frees.get(("cb", 0))
    frees[("dg", 1)] = frees.get(("cb", 1))
    frees[("dg", 2)] = (cDVE.s, m_x2bf[0])
    frees[("dg", 3)] = (cDVE.s, m_x2bf[0])
    Y0_O = 120832
    for _i in range(4):
        hi_t = (GUP_O + (_i + 1) * 8192 - Y0_O + 2047) // 2048 - 1
        if hi_t >= 0:
            gup_free[_i] = frees[("y0s", 0, min(hi_t, NEC - 1))]
    gup_free["pe"] = (cPE.s, m_sums)
    dwp_free[0] = (cDVE.s, m_x2bf[1])
    dwp_free[1] = (cDVE.s, m_x2bf[1])
    frees["hfree"] = (dS.s, x2s_done)

    x2r_free = {}
    outs_free = {}
    m_xsb_c = {}
    for qc in range(2):
        for ce in range(NEC):
            S.op("pe", "matmul", bank(6, parts=1), ones_c,
                 x2bf[:, ce * OWN + qc * 512: ce * OWN + qc * 512 + 512],
                 start=(ce == 0), stop=(ce == NEC - 1),
                 wait=[frees.get(("x2t", qc, ce))] +
                      ([frees.get("b6d")] if ce == 0 else []),
                 inc=cPE.inc() if ce == NEC - 1 else None)
        m_xs = cPE.n
        S.op("dve", "tensor_scalar_mul", mean2, bank(6, parts=1), -128.0,
             wait=[(cPE.s, m_xs)])
        S.op("dve", "tensor_copy", shi2, mean2)
        S.op("dve", "tensor_sub", slo2, mean2, shi2, inc=cDVE.inc())
        m_hl = cDVE.n
        S.op("pe", "matmul", bank(7), ones_r, shi2, start=True, stop=False,
             wait=[(cDVE.s, m_hl), frees.get("b7d")])
        S.op("pe", "matmul", bank(7), ones_r, slo2, start=False, stop=True,
             inc=cPE.inc())
        m_bc = cPE.n
        S.op("dve", "tensor_copy", xsb[qc], bank(7),
             wait=[(cPE.s, m_bc)], inc=cDVE.inc())
        m_xsb_c[qc] = cDVE.n
        frees["b6d"] = (cDVE.s, m_xsb_c[qc])
        frees["b7d"] = (cDVE.s, m_xsb_c[qc])

    for qc in range(2):
        m_xsb = m_xsb_c[qc]

        for ic in range(NIC):
            ip = (ic // 2) % 2
            sub = (ic % 2) * 2048
            borrow = (qc == 1 and ic < 4)
            if borrow:
                gpan = dwp[ic // 2][:, 0:4096]
                upan = dwp[ic // 2][:, 4096:8192]
            else:
                gpan = gup[ip]
                upan = gup[2 + ip]
            if ic % 2 == 0:
                if borrow:
                    gw_wait = [dwp_free.get(ic // 2)]
                    uw_wait = []
                else:
                    gw_wait = [gup_free.get(ip), gup_free.get("pe")]
                    uw_wait = [gup_free.get(2 + ip)]
                S.op("pool", "dma_start",
                     out=gpan.rearrange("p (g c) -> p g c", g=2),
                     in_=gw_d[ic:ic + 2].transpose([1, 0, 2]),
                     wait=gw_wait, inc=dW.inc(16))
                S.op("pool", "dma_start",
                     out=upan.rearrange("p (g c) -> p g c", g=2),
                     in_=uw_d[ic:ic + 2].transpose([1, 0, 2]),
                     wait=uw_wait, inc=dW.inc(16))
            w_dw = dW.n
            bg, bu = ic % 2, 2 + ic % 2
            for ch in range(NEC):
                S.op("pe", "matmul", bank(bg),
                     gpan[:, sub + ch * 128: sub + (ch + 1) * 128],
                     x2bf[:, ch * OWN + qc * 512: ch * OWN + qc * 512 + 512],
                     start=(ch == 0), stop=(ch == NEC - 1),
                     wait=[(dW.s, w_dw), frees.get(("dg", bg)),
                           frees.get(("db", bg))] if ch == 0 else ())
            for ch in range(NEC):
                S.op("pe", "matmul", bank(bu),
                     upan[:, sub + ch * 128: sub + (ch + 1) * 128],
                     x2bf[:, ch * OWN + qc * 512: ch * OWN + qc * 512 + 512],
                     start=(ch == 0), stop=(ch == NEC - 1),
                     wait=[frees.get(("dg", bu)),
                           frees.get(("db", bu))] if ch == 0 else (),
                     inc=cPE.inc() if ch == NEC - 1 else None)
            m_gu = cPE.n
            if borrow and ic % 2 == 1:
                dwp_free[ic // 2] = (cPE.s, m_gu)
            S.op("dve", "tensor_add", tsc1, bank(bg), xsb[qc],
                 wait=[(cPE.s, m_gu), (cDVE.s, m_xsb), frees.get("tsc1")])
            S.op("dve", "tensor_add", tsc2, bank(bu), xsb[qc],
                 wait=[frees.get("tsc2")], inc=cDVE.inc())
            m_t = cDVE.n
            frees[("dg", bg)] = (cDVE.s, m_t)
            frees[("dg", bu)] = (cDVE.s, m_t)
            if ic % 2 == 1 and not borrow:
                gup_free[ip] = (cPE.s, m_gu)
                gup_free[2 + ip] = (cPE.s, m_gu)
            S.op("act", "activation", sgs, tsc1, AF.Silu,
                 scale=gs_s[:, ic:ic + 1],
                 wait=[(cDVE.s, m_t), frees.get("sgs")], inc=cACT.inc())
            m_sg = cACT.n
            frees["tsc1"] = (cACT.s, m_sg)
            S.op("dve", "scalar_tensor_tensor", hS[:, ic * 512:(ic + 1) * 512],
                 tsc2, us_s[:, ic:ic + 1], sgs, ALU.mult, ALU.mult,
                 wait=[(cACT.s, m_sg), frees.get("hfree")], inc=cDVE.inc())
            frees["tsc2"] = (cDVE.s, cDVE.n)
            frees["sgs"] = (cDVE.s, cDVE.n)
            frees[("hdone", ic)] = (cDVE.s, cDVE.n)
            if ic >= 2:
                jc = ic - 2
                S.op("pe", "matmul", bank(6, parts=1), ones_c,
                     hS[:, jc * 512:(jc + 1) * 512],
                     start=(jc == 0), stop=False, skip_group_check=True,
                     wait=[frees.get(("hdone", jc)),
                           frees.get("b6d") if jc == 0 else None])

        for jc in (NIC - 2, NIC - 1):
            S.op("pe", "matmul", bank(6, parts=1), ones_c,
                 hS[:, jc * 512:(jc + 1) * 512],
                 start=False, stop=(jc == NIC - 1), skip_group_check=True,
                 wait=[frees.get(("hdone", jc))],
                 inc=cPE.inc() if jc == NIC - 1 else None)
        m_hs = cPE.n
        S.op("dve", "tensor_scalar_mul", mean2, bank(6, parts=1), -128.0,
             wait=[(cPE.s, m_hs)])
        S.op("dve", "tensor_copy", shi2, mean2)
        S.op("dve", "tensor_sub", slo2, mean2, shi2, inc=cDVE.inc())
        m_hl2 = cDVE.n
        S.op("pe", "matmul", bank(7), ones_r, shi2, start=True, stop=False,
             wait=[(cDVE.s, m_hl2), frees.get("b7d")])
        S.op("pe", "matmul", bank(7), ones_r, slo2, start=False, stop=True,
             inc=cPE.inc())
        m_bc2 = cPE.n
        S.op("dve", "tensor_copy", xsb2, bank(7),
             wait=[(cPE.s, m_bc2), frees.get("xsb2")], inc=cDVE.inc())
        m_xsb2 = cDVE.n
        frees["b6d"] = (cDVE.s, m_xsb2)
        frees["b7d"] = (cDVE.s, m_xsb2)

        def d_stats(ce):
            dp = ce % 2
            S.op("pe", "matmul", bank(2, parts=1), ones_c, y2bfr[dp],
                 start=(ce == 0), stop=(ce == NEC - 1),
                 wait=[frees[("my2", ce)]] + ([frees.get("lnb2")] if ce == 0 else []))
            S.op("pe", "matmul", bank(3, parts=1), ones_c, sq2[dp],
                 start=(ce == 0), stop=(ce == NEC - 1),
                 inc=cPE.inc())
            frees[("y2bf", dp)] = (cPE.s, cPE.n)
            frees[("sq2", dp)] = (cPE.s, cPE.n)

        for ce in range(NEC):
            dp = ce % 2
            S.op("pool", "dma_start", out=dwp[dp], in_=dw_d[ce],
                 wait=[dwp_free.get(dp)], inc=dW.inc(16))
            d_dw = dW.n
            S.op("sync", "dma_start", out=x2r[dp],
                 in_=x2scr[ce * 128:(ce + 1) * 128, qc * 512:(qc + 1) * 512],
                 wait=[(dS.s, x2s_done), x2r_free.get(dp)], inc=dX.inc(16))
            d_dx = dX.n
            bp = ce % 2
            for ic in range(NIC):
                S.op("pe", "matmul", bank(bp),
                     dwp[dp][:, ic * 128:(ic + 1) * 128],
                     hS[:, ic * 512:(ic + 1) * 512],
                     start=(ic == 0), stop=(ic == NIC - 1),
                     wait=[(dW.s, d_dw), frees.get(("db", bp)),
                           frees.get(("dg", bp))] if ic == 0 else (),
                     inc=cPE.inc() if ic == NIC - 1 else None)
            m_dn = cPE.n
            dwp_free[dp] = (cPE.s, m_dn)
            yv = y2t(ce)
            S.op("dve", "tensor_add", tsc1, bank(bp), xsb2,
                 wait=[(cPE.s, m_dn), (cDVE.s, m_xsb2), frees.get("tsc1")])
            S.op("dve", "scalar_tensor_tensor", yv, tsc1, ds_s[:, ce:ce + 1],
                 x2r[dp], ALU.mult, ALU.add,
                 wait=[(dX.s, d_dx), frees.get(("y2", qc, ce))])
            frees[("db", bp)] = (cDVE.s, cDVE.n + 1)
            frees["tsc1"] = (cDVE.s, cDVE.n + 1)
            S.op("dve", "tensor_copy", y2bfr[dp], yv,
                 wait=[frees.get(("y2bf", dp))])
            S.op("dve", "tensor_mul", sq2[dp], y2bfr[dp], y2bfr[dp],
                 wait=[frees.get(("sq2", dp))], inc=cDVE.inc())
            m_y2 = cDVE.n
            frees[("my2", ce)] = (cDVE.s, m_y2)
            x2r_free[dp] = (cDVE.s, m_y2)
            if ce >= 1:
                d_stats(ce - 1)
        d_stats(NEC - 1)
        m_sums2 = cPE.n
        m_bcast2 = ln_stats_tail(2, 3, m_sums2,
                                 (mean2, var2, rstd2, mr2, shi2, slo2, mhi2, mlo2),
                                 (rb2, mrb2, 4, 5))
        frees["lnb2"] = (cDVE.s, m_bcast2)
        frees["xsb2"] = (cDVE.s, m_bcast2)
        for ce in range(NEC):
            op = ce % 2
            yv = y2t(ce)
            eng = "pool" if (qc == 1 and ce % 2 == 1) else "dve"
            ctr = cPOOL if eng == "pool" else cDVE
            S.op(eng, "tensor_mul", outs[op], yv, rb2,
                 wait=[(cDVE.s, m_bcast2), outs_free.get(op)])
            S.op(eng, "tensor_sub", outs[op], outs[op], mrb2,
                 inc=ctr.inc())
            m_o = ctr.n
            m_o_eng = ctr.s
            frees[("y2", qc + 1, ce)] = (m_o_eng, m_o)
            if ce % 4 == 3:
                gup_free[ce // 4] = (m_o_eng, m_o)
            S.op("sync", "dma_start",
                 out=out_d[ce * 128:(ce + 1) * 128, qc * 512:(qc + 1) * 512],
                 in_=outs[op], wait=[(m_o_eng, m_o)], inc=dS.inc(16))
            outs_free[op] = (dS.s, dS.n)
        frees["hfree"] = (cPE.s, cPE.n)
        frees[("dg", 2)] = (cDVE.s, cDVE.n)
        frees[("dg", 3)] = (cDVE.s, cDVE.n)

    S.raw("sync", lambda e, s=dS.s, v=dS.n: e.wait_ge(s, v))

    with nc.Block() as blk:
        blk.tensor(lambda e: [f(e) for f in S.q["pe"]])
        blk.scalar(lambda e: [f(e) for f in S.q["act"]])
        blk.vector(lambda e: [f(e) for f in S.q["dve"]])
        blk.gpsimd(lambda e: [f(e) for f in S.q["pool"]])
        blk.sync(lambda e: [f(e) for f in S.q["sync"]])
    return nc


def _rope_tables():
    half = DH // 2
    e = np.arange(half, dtype=np.float32) / np.float32(half)
    inv = (np.float32(1.0) / np.power(np.float32(10000.0), e)).astype(np.float32)
    ang = (np.arange(SEQ, dtype=np.float32)[:, None] * inv[None, :]).astype(np.float32)
    ang = np.concatenate([ang, ang], axis=1)
    return (np.ascontiguousarray(np.cos(ang).T.astype(np.float32)),
            np.ascontiguousarray(np.sin(ang).T.astype(np.float32)))


def _prep(inputs):
    x = np.asarray(inputs["L0_attn_hidden_states"], dtype=np.float32)[0]
    xt_full = np.ascontiguousarray(x.T)
    wq = np.asarray(inputs["L0_attn_wq"], dtype=np.float32)
    wk = np.asarray(inputs["L0_attn_wk"], dtype=np.float32)
    wv = np.asarray(inputs["L0_attn_wv"], dtype=np.float32)
    wo = np.asarray(inputs["L0_attn_wo"], dtype=np.float32)
    qsc = np.asarray(inputs["L0_attn_query_scale"], dtype=np.float32).reshape(128, 1)
    ksc = np.asarray(inputs["L0_attn_key_scale"], dtype=np.float32).reshape(128, 1)
    def u8(a):
        a = np.ascontiguousarray(np.asarray(a, dtype=np.int32))
        return np.ascontiguousarray(a.view(np.uint8).reshape(a.shape + (4,))[..., 0])
    gw = u8(inputs["L0_ffn_gate_w"])
    uw = u8(inputs["L0_ffn_up_w"])
    dw = u8(inputs["L0_ffn_down_w"])
    gsc = np.asarray(inputs["L0_ffn_gate_scale"], dtype=np.float32)
    usc = np.asarray(inputs["L0_ffn_up_scale"], dtype=np.float32)
    dsc = np.asarray(inputs["L0_ffn_down_scale"], dtype=np.float32)

    shared = {
        "wq_t": np.ascontiguousarray(wq.reshape(NEC, 128, NH, 128).transpose(2, 1, 0, 3).reshape(NH, 128, NEC * 128)),
        "wk_t": np.ascontiguousarray(wk.reshape(NEC, 128, NKV, 128).transpose(2, 1, 0, 3).reshape(NKV, 128, NEC * 128)),
        "wv_t": np.ascontiguousarray(wv.reshape(NEC, 128, 512)),
        "wo_t": np.ascontiguousarray(wo.reshape(NEC, 128, NEC, 128).transpose(2, 1, 0, 3).reshape(NEC, 128, NEC * 128)),
        "gw_t": np.ascontiguousarray(gw.reshape(NIC, 128, NEC, 128).transpose(0, 3, 2, 1).reshape(NIC, 128, NEC * 128)),
        "uw_t": np.ascontiguousarray(uw.reshape(NIC, 128, NEC, 128).transpose(0, 3, 2, 1).reshape(NIC, 128, NEC * 128)),
        "dw_t": np.ascontiguousarray(dw.reshape(NEC, 128, NIC, 128).transpose(0, 3, 2, 1).reshape(NEC, 128, NIC * 128)),
        "gs_t": np.ascontiguousarray(gsc.reshape(NIC, 128).T),
        "us_t": np.ascontiguousarray(usc.reshape(NIC, 128).T),
        "ds_t": np.ascontiguousarray(dsc.reshape(NEC, 128).T),
        "qsc": qsc, "ksc": ksc,
    }
    kk = np.arange(128)[:, None]
    qq = np.arange(128)[None, :]
    tpos = (kk <= qq).astype(ml_dtypes.bfloat16)
    tneg = (kk >= qq).astype(ml_dtypes.bfloat16)
    shared["tri"] = np.ascontiguousarray(np.concatenate([tpos, tneg], axis=1))
    rs = np.zeros((16, NH * 128), dtype=ml_dtypes.bfloat16)
    for h in range(NH):
        rs[h, h * 128:(h + 1) * 128] = 1
    shared["rsel"] = rs

    cosT, sinT = _rope_tables()
    maps = []
    for cid in range(NCORE):
        hs = cid * OWN - 1024
        xt_c = np.zeros((E, HALO), np.float32)
        lo, hi = max(0, hs), min(SEQ, hs + HALO)
        xt_c[:, lo - hs:hi - hs] = xt_full[:, lo:hi]
        pos = np.clip(np.arange(hs, hs + HALO), 0, SEQ - 1)
        zs = np.zeros((128, NKT * 31), dtype=ml_dtypes.bfloat16)
        valid = ((np.arange(hs, hs + HALO) >= 0) & (np.arange(hs, hs + HALO) < SEQ))
        for t in range(NKT):
            zs[:, t * 31 + 15] = valid[t * 128:(t + 1) * 128].astype(ml_dtypes.bfloat16)
        m = dict(shared)
        m["xt"] = xt_c
        m["cos_t"] = np.ascontiguousarray(cosT[:, pos])
        m["sin_t"] = np.ascontiguousarray(sinT[:, pos])
        m["zsel"] = zs
        maps.append(m)
    return maps


_CACHE = {}


def _get_program(taps=()):
    key = tuple(sorted(taps))
    if key not in _CACHE:
        _CACHE[key] = build_program(taps)
    return _CACHE[key]


def kernel(**inputs):
    nc = _get_program()
    maps = _prep(inputs)
    res = run_bass_kernel_spmd(nc, maps, core_ids=list(range(NCORE)))
    out = np.concatenate([r["out_t"] for r in res.results], axis=1)
    return np.ascontiguousarray(out.T).reshape(1, SEQ, E)


# version 21 (final)
# speedup vs baseline: 1.0852x; 1.0080x over previous
import numpy as np
import ml_dtypes
import concourse.bass as bass
import concourse.mybir as mybir
from concourse.bass_utils import run_bass_kernel_spmd

F32 = mybir.dt.float32
BF16 = mybir.dt.bfloat16
U8 = mybir.dt.uint8
AF = mybir.ActivationFunctionType
ALU = mybir.AluOpType

NCORE = 8
SEQ = 8192
E = 2048
OWN = 1024
HALO = 3072
NH = 16
NKV = 4
DH = 128
NI = 8192
WIN = 1024
CAP = 50.0
NEC = E // 128
NIC = NI // 128
NKT = HALO // 128
EXPB = -25.0

BPP = 212863


def _al(n, a=64):
    return (n + a - 1) // a * a


def _blocks(qc):
    Q0 = 1024 + qc * 512
    out = []
    for o in range(-WIN, WIN + 512 - 128 + 1, 128):
        js = [j for j in range(4) if abs(o - 128 * j) <= WIN]
        if not js:
            continue
        tris = [(j, +1) for j in js if o - 128 * j == WIN] + \
               [(j, -1) for j in js if o - 128 * j == -WIN]
        out.append((Q0 + o, js[0], js[-1], tris))
    assert len(out) == 20
    return out


class Ctr:
    def __init__(self, nc, name):
        self.s = nc.alloc_semaphore(name)
        self.n = 0

    def inc(self, k=1):
        self.n += k
        return (self.s, k)


class Sch:
    ENGS = ("pe", "act", "dve", "pool", "sync")

    def __init__(self, nc):
        self.nc = nc
        self.q = {e: [] for e in self.ENGS}
        self.hw = {e: {} for e in self.ENGS}

    def op(self, eng, meth, *args, wait=(), inc=None, **kw):
        real_waits = []
        for w in wait:
            if w is None:
                continue
            s, v = w
            if self.hw[eng].get(s.name, -1) < v:
                self.hw[eng][s.name] = v
                real_waits.append((s, v))
        if inc is not None:
            s, _ = inc
        def f(e, m=meth, a=args, k=kw, rw=tuple(real_waits), inc=inc):
            for (s, v) in rw:
                e.wait_ge(s, v)
            ins = getattr(e, m)(*a, **k)
            if inc is not None:
                ins.then_inc(inc[0], inc[1])
        self.q[eng].append(f)

    def raw(self, eng, fn):
        self.q[eng].append(fn)

    def barrier(self, ctrs):
        vals = {}
        for e in self.ENGS:
            c = ctrs[e]
            tgt = c.inc(1)
            vals[e] = (c.s, c.n)
            self.raw(e, lambda eng, s=tgt[0]: eng.drain().then_inc(s, 1))
        for e in self.ENGS:
            for o in self.ENGS:
                if o == e:
                    continue
                s, v = vals[o]
                if self.hw[e].get(s.name, -1) < v:
                    self.hw[e][s.name] = v
                    self.raw(e, lambda eng, s=s, v=v: eng.wait_ge(s, v))


def build_program(taps=()):
    nc = bass.Bass("TRN2", target_bir_lowering=False, debug=False,
                   detect_race_conditions=False)

    def din(name, shape, dt):
        return nc.dram_tensor(name, list(shape), dt, kind="ExternalInput").ap()

    xt = din("xt", (E, HALO), F32)
    cos_t = din("cos_t", (DH, HALO), F32)
    sin_t = din("sin_t", (DH, HALO), F32)
    zsel_d = din("zsel", (128, NKT * 31), BF16)
    tri_d = din("tri", (128, 256), BF16)
    rsel_d = din("rsel", (16, NH * 128), BF16)
    wq_d = din("wq_t", (NH, 128, NEC * 128), F32)
    wk_d = din("wk_t", (NKV, 128, NEC * 128), F32)
    wv_d = din("wv_t", (NEC, 128, 512), F32)
    wo_d = din("wo_t", (NEC, 128, NEC * 128), F32)
    gw_d = din("gw_t", (NIC, 128, NEC * 128), U8)
    uw_d = din("uw_t", (NIC, 128, NEC * 128), U8)
    dw_d = din("dw_t", (NEC, 128, NIC * 128), U8)
    gs_d = din("gs_t", (128, NIC), F32)
    us_d = din("us_t", (128, NIC), F32)
    ds_d = din("ds_t", (128, NEC), F32)
    qsc_d = din("qsc", (128, 1), F32)
    ksc_d = din("ksc", (128, 1), F32)

    out_d = nc.dram_tensor("out_t", [E, OWN], F32, kind="ExternalOutput").ap()
    x2k = "ExternalOutput" if "x2" in taps else "Internal"
    x2scr = nc.dram_tensor("x2scr", [E, OWN], F32, kind=x2k).ap()
    dbg = {}
    if "qkv" in taps:
        dbg["k"] = nc.dram_tensor("dbg_k", [128, NKV * HALO], BF16, kind="ExternalOutput").ap()
        dbg["v"] = nc.dram_tensor("dbg_v", [128, NKT * 512], BF16, kind="ExternalOutput").ap()
        dbg["q"] = nc.dram_tensor("dbg_q", [128, NH * OWN], BF16, kind="ExternalOutput").ap()
    if "attn" in taps:
        dbg["attn"] = nc.dram_tensor("dbg_attn", [128, NH * OWN], BF16, kind="ExternalOutput").ap()

    ARENA_B = (BPP - 256) // 64 * 64
    AR = nc.alloc_sbuf_tensor("arena", [128, ARENA_B], U8).ap()

    def view(off, parts, cols, dt):
        sz = {F32: 4, BF16: 2, U8: 1}[dt]
        nb = cols * sz
        assert off % 64 == 0 and off + nb <= ARENA_B, (off, nb)
        return AR[0:parts, off:off + nb].bitcast(dt)

    o = 0
    zsel = view(o, 128, NKT * 31, BF16); o = _al(o + NKT * 31 * 2)
    tri = view(o, 128, 256, BF16); o = _al(o + 512)
    rsel = view(o, 16, NH * 128, BF16); o = _al(o + NH * 128 * 2)
    gs_s = view(o, 128, NIC, F32); o = _al(o + NIC * 4)
    us_s = view(o, 128, NIC, F32); o = _al(o + NIC * 4)
    ds_s = view(o, 128, NEC, F32); o = _al(o + NEC * 4)
    qsc_s = view(o, 128, 1, F32); o = _al(o + 4)
    ksc_s = view(o, 128, 1, F32); o = _al(o + 4)
    neg25 = view(o, 128, 1, F32); o = _al(o + 4)
    eps12 = view(o, 128, 1, F32); o = _al(o + 4)
    eps5 = view(o, 128, 1, F32); o = _al(o + 4)
    ones_c = view(o, 128, 1, BF16); o = _al(o + 2)
    ones_r = view(o, 1, 128, BF16); o = _al(o + 256)
    P_END = o
    assert P_END < 8192

    a = P_END
    def Av(parts, cols, dt):
        nonlocal a
        v = view(a, parts, cols, dt)
        a = _al(a + cols * {F32: 4, BF16: 2, U8: 1}[dt])
        return v
    x_own = Av(128, NEC * OWN, BF16)
    xb = [Av(128, NEC * 512, BF16) for _ in range(2)]
    wpA = [Av(128, NEC * 128, BF16) for _ in range(2)]
    wvp = Av(128, NEC * 512, BF16)
    cosb = [Av(128, 512, F32) for _ in range(2)]
    sinb = [Av(128, 512, F32) for _ in range(2)]
    TOPS = 169984 + NH * OWN * 2
    rot_o = [a, TOPS]
    rot = [Av(128, 512, F32), view(TOPS, 128, 512, F32)]
    qa = [Av(128, 512, F32), view(TOPS + 2048, 128, 512, F32)]
    qr = [Av(128, 512, F32), view(TOPS + 4096, 128, 512, F32),
          view(TOPS + 6144, 128, 512, F32)]
    sqb = [view(rot_o[i], 128, 512, BF16) for i in range(2)]
    rnrow = [Av(1, 512, F32) for _ in range(2)]
    rnhi = [Av(1, 512, BF16) for _ in range(2)]
    rnlo = [Av(1, 512, BF16) for _ in range(2)]
    assert a <= 120832, a

    KO, VO, QO = 120832, 145408, 169984
    K_fm = view(KO, 128, NKV * HALO, BF16)
    V_tm = view(VO, 128, NKT * 512, BF16)
    Q_fm = view(QO, 128, NH * OWN, BF16)
    wkp = view(QO, 128, NKV * NEC * 128, BF16)
    assert QO + NH * OWN * 2 <= ARENA_B

    b = P_END
    def Bv(parts, cols, dt):
        nonlocal b
        v = view(b, parts, cols, dt)
        b = _al(b + cols * {F32: 4, BF16: 2, U8: 1}[dt])
        return v
    pt = [Bv(128, 1024, BF16) for _ in range(3)]
    th = [Bv(128, 1024, F32) for _ in range(3)]
    rzf = Bv(16, 512, F32)
    rzhi = Bv(16, 512, BF16)
    rzlo = Bv(16, 512, BF16)
    assert b <= 84992, b
    attn = view(84992, 128, NH * OWN, BF16)

    x2bf = view(P_END, 128, NEC * OWN, BF16)
    c = _al(P_END + NEC * OWN * 2)
    def Cv(parts, cols, dt):
        nonlocal c
        v = view(c, parts, cols, dt)
        c = _al(c + cols * {F32: 4, BF16: 2, U8: 1}[dt])
        return v
    wpC = [Cv(128, NEC * 128, BF16) for _ in range(2)]
    xres = [Cv(128, 512, F32) for _ in range(4)]
    rbsb = Cv(128, 512, F32)
    mrbsb = Cv(128, 512, F32)
    sqc = [Cv(128, 512, BF16) for _ in range(4)]
    ybfr = [Cv(128, 512, BF16) for _ in range(4)]
    meanr = Cv(1, 512, F32)
    varr = Cv(1, 512, F32)
    rstdr = Cv(1, 512, F32)
    mrr = Cv(1, 512, F32)
    shi = Cv(1, 512, BF16)
    slo = Cv(1, 512, BF16)
    mhi = Cv(1, 512, BF16)
    mlo = Cv(1, 512, BF16)
    assert c <= 84992, c
    yC = view(120832, 128, NEC * 512, F32)

    d = _al(P_END + NEC * OWN * 2)
    def Dv(parts, cols, dt):
        nonlocal d
        v = view(d, parts, cols, dt)
        d = _al(d + cols * {F32: 4, BF16: 2, U8: 1}[dt])
        return v
    dwp = [Dv(128, NIC * 128, BF16) for _ in range(2)]
    outs = [Dv(128, 512, F32) for _ in range(2)]
    x2r = [Dv(128, 512, F32) for _ in range(2)]
    xsb = [Dv(128, 512, F32) for _ in range(2)]
    xsb2 = Dv(128, 512, F32)
    tsc1 = Dv(128, 512, F32)
    tsc2 = Dv(128, 512, F32)
    sgs = Dv(128, 512, F32)
    y2bfr = [Dv(128, 512, BF16) for _ in range(2)]
    mean2 = Dv(1, 512, F32)
    var2 = Dv(1, 512, F32)
    rstd2 = Dv(1, 512, F32)
    mr2 = Dv(1, 512, F32)
    shi2 = Dv(1, 512, BF16)
    slo2 = Dv(1, 512, BF16)
    mhi2 = Dv(1, 512, BF16)
    mlo2 = Dv(1, 512, BF16)
    rb2 = Dv(128, 512, F32)
    mrb2 = Dv(128, 512, F32)
    sq2 = [Dv(128, 512, BF16) for _ in range(2)]
    GUP_O = _al(d)
    assert GUP_O + 32 * 1024 + NIC * 512 * 2 <= ARENA_B, GUP_O
    gup = [view(GUP_O + i * 8192, 128, 2 * NEC * 128, BF16) for i in range(4)]
    y2a = view(GUP_O, 128, 8 * 512, F32)
    y2b = view(GUP_O + 16384, 128, 8 * 512, F32)
    HS_O = GUP_O + 32768
    hS = view(HS_O, 128, NIC * 512, BF16)

    def y2t(ce):
        return (y2a if ce < 8 else y2b)[:, (ce % 8) * 512:(ce % 8) * 512 + 512]

    PS = nc.alloc_psum_tensor("PS", [128, 4096], F32).ap()

    def bank(i, n=1, parts=128):
        return PS[0:parts, i * 512:(i + n) * 512]

    S = Sch(nc)
    dW = Ctr(nc, "dW")
    dX = Ctr(nc, "dX")
    dS = Ctr(nc, "dS")
    cPE = Ctr(nc, "cPE")
    cACT = Ctr(nc, "cACT")
    cDVE = Ctr(nc, "cDVE")
    cPOOL = Ctr(nc, "cPOOL")
    cSYNC = Ctr(nc, "cSYNC")
    CTRS = {"pe": cPE, "act": cACT, "dve": cDVE, "pool": cPOOL, "sync": cSYNC}

    S.op("sync", "dma_start", out=zsel, in_=zsel_d, inc=dX.inc(16))
    S.op("sync", "dma_start", out=tri, in_=tri_d, inc=dX.inc(16))
    S.op("sync", "dma_start", out=rsel, in_=rsel_d, inc=dX.inc(16))
    S.op("sync", "dma_start", out=gs_s, in_=gs_d, inc=dX.inc(16))
    S.op("sync", "dma_start", out=us_s, in_=us_d, inc=dX.inc(16))
    S.op("sync", "dma_start", out=ds_s, in_=ds_d, inc=dX.inc(16))
    S.op("sync", "dma_start", out=qsc_s, in_=qsc_d, inc=dX.inc(16))
    S.op("sync", "dma_start", out=ksc_s, in_=ksc_d, inc=dX.inc(16))
    preload_dx = dX.n
    S.op("dve", "memset", neg25, EXPB)
    S.op("dve", "memset", eps12, 1e-12)
    S.op("dve", "memset", eps5, 1e-5)
    S.op("dve", "memset", ones_c, 1.0)
    S.op("dve", "memset", ones_r, 1.0, inc=cDVE.inc())
    m_pre_dve = cDVE.n

    def xv(cin, T):
        if T in (2, 3):
            return x_own[:, cin * OWN + (T - 2) * 512: cin * OWN + (T - 2) * 512 + 512]
        return xb[T % 2][:, cin * 512: cin * 512 + 512]

    frees = {}

    chain_i = [0]

    def chain_front(ps_pp, cosv, sinv, m_proj):
        ci = chain_i[0]
        chain_i[0] += 1
        s = ci % 2
        s3 = ci % 3
        ps = bank(ps_pp)
        S.op("act", "activation", rot[s][0:64, :], ps[64:128, :], AF.Copy, scale=-1.0,
             wait=[(cPE.s, m_proj), frees.get(("rot", s)), frees.get(("sqb", s))])
        S.op("act", "activation", rot[s][64:128, :], ps[0:64, :], AF.Copy,
             inc=cACT.inc())
        m_rot = cACT.n
        S.op("dve", "tensor_mul", qa[s], ps, cosv, wait=[(cACT.s, m_rot)])
        S.op("dve", "tensor_mul", rot[s], rot[s], sinv)
        S.op("dve", "tensor_add", qr[s3], qa[s], rot[s],
             wait=[frees.get(("qr", s3))])
        S.op("dve", "tensor_mul", sqb[s], qr[s3], qr[s3],
             inc=cDVE.inc())
        m_qr = cDVE.n
        frees[("rot", s)] = (cDVE.s, m_qr)
        return dict(s=s, s3=s3, r=s % 2, m_qr=m_qr)

    def chain_tail1(ctx):
        s, r, m_qr = ctx["s"], ctx["r"], ctx["m_qr"]
        S.op("pe", "matmul", bank(3, parts=1), ones_c, sqb[s], start=True, stop=True,
             wait=[(cDVE.s, m_qr), frees.get("b3")], inc=cPE.inc())
        m_ss = cPE.n
        frees[("sqb", s)] = (cPE.s, m_ss)
        S.op("act", "activation", rnrow[r], bank(3, parts=1), AF.Ln,
             bias=eps12[0:1, :],
             wait=[(cPE.s, m_ss), frees.get(("rnrow", r))])
        S.op("act", "activation", rnrow[r], rnrow[r], AF.Exp, scale=-0.5,
             inc=cACT.inc())
        m_rn = cACT.n
        frees["b3"] = (cACT.s, m_rn)
        S.op("dve", "tensor_copy", rnhi[r], rnrow[r],
             wait=[(cACT.s, m_rn), frees.get(("rows", r))])
        S.op("dve", "tensor_sub", rnlo[r], rnrow[r], rnhi[r], inc=cDVE.inc())
        m_rows = cDVE.n
        frees[("rnrow", r)] = (cDVE.s, m_rows)
        ctx["m_rows"] = m_rows

    def chain_tail2(ctx, sc_ap, dest):
        s3, r, m_rows = ctx["s3"], ctx["r"], ctx["m_rows"]
        S.op("pe", "matmul", bank(2), ones_r, rnhi[r], start=True, stop=False,
             wait=[(cDVE.s, m_rows), frees.get("b2")])
        S.op("pe", "matmul", bank(2), ones_r, rnlo[r], start=False, stop=True,
             inc=cPE.inc())
        m_r1 = cPE.n
        frees[("rows", r)] = (cPE.s, m_r1)
        S.op("dve", "scalar_tensor_tensor", dest, qr[s3], sc_ap, bank(2),
             ALU.mult, ALU.mult, wait=[(cPE.s, m_r1)], inc=cDVE.inc())
        m_fin = cDVE.n
        frees[("qr", s3)] = (cDVE.s, m_fin)
        frees["b2"] = (cDVE.s, m_fin)
        return m_fin


    nproj = [0]
    projfree = {}
    PROJ_BANKS = (0, 1, 6, 7)

    def do_proj(lhs_tiles, rhs_tiles, extra_wait=()):
        pp = PROJ_BANKS[nproj[0] % 4]
        nproj[0] += 1
        w = [projfree.get(pp)] + list(extra_wait)
        for ci in range(NEC):
            S.op("pe", "matmul", bank(pp), lhs_tiles(ci), rhs_tiles(ci),
                 start=(ci == 0), stop=(ci == NEC - 1),
                 wait=w if ci == 0 else (),
                 inc=cPE.inc() if ci == NEC - 1 else None)
        return pp, cPE.n

    xt_pc = xt.rearrange("(c p) t -> p c t", p=128)
    xb_free = {}
    pend = []

    def _advance(new):
        pend.append(new)
        if len(pend) >= 2:
            c = pend[-2]
            if "m_rows" not in c[0]:
                chain_tail1(c[0])
        if len(pend) >= 3:
            c = pend.pop(0)
            chain_tail2(c[0], c[1], c[2])

    def _flush_all():
        while pend:
            c = pend.pop(0)
            if "m_rows" not in c[0]:
                chain_tail1(c[0])
            chain_tail2(c[0], c[1], c[2])

    wk_dw = [0]
    wv_dw = [0]
    for T in range(6):
        dst_is_own = T in (2, 3)
        if dst_is_own:
            dst3 = x_own.rearrange("p (c t) -> p c t", c=NEC)[
                :, :, (T - 2) * 512:(T - 2) * 512 + 512]
        else:
            dst3 = xb[T % 2].rearrange("p (c t) -> p c t", c=NEC)
        S.op("pool", "dma_start", out=dst3,
             in_=xt_pc[:, :, T * 512:(T + 1) * 512],
             wait=[xb_free.get(T % 2)] if not dst_is_own else (),
             inc=dW.inc(16))
        x_dw = dW.n
        if T == 0:
            wk_g_dw = []
            for g in range(NKV):
                S.op("pool", "dma_start",
                     out=wkp[:, g * 2048:(g + 1) * 2048],
                     in_=wk_d[g], inc=dW.inc(16))
                wk_g_dw.append(dW.n)
            wk_dw[0] = wk_g_dw
            S.op("pool", "dma_start", out=wvp.rearrange("p (g c) -> p g c", g=NEC),
                 in_=wv_d.transpose([1, 0, 2]), inc=dW.inc(16))
            wv_dw[0] = dW.n
        S.op("sync", "dma_start", out=cosb[T % 2], in_=cos_t[:, T * 512:(T + 1) * 512],
             wait=[frees.get(("cos", T % 2))], inc=dX.inc(16))
        S.op("sync", "dma_start", out=sinb[T % 2], in_=sin_t[:, T * 512:(T + 1) * 512],
             inc=dX.inc(16))
        cs_dx = dX.n
        for g in range(NKV):
            pp, m_proj = do_proj(
                lambda ci, g=g: wkp[:, g * 2048 + ci * 128: g * 2048 + ci * 128 + 128],
                lambda ci, T=T: xv(ci, T),
                extra_wait=[(dW.s, max(x_dw, wk_dw[0][g])), (dX.s, cs_dx),
                            (dX.s, preload_dx), (cDVE.s, m_pre_dve)])
            ctx = chain_front(pp, cosb[T % 2], sinb[T % 2], m_proj)
            projfree[pp] = (cDVE.s, ctx["m_qr"])
            frees[("cos", T % 2)] = (cDVE.s, ctx["m_qr"])
            _advance((ctx, ksc_s,
                      K_fm[:, g * HALO + T * 512: g * HALO + T * 512 + 512]))
        for sub in range(4):
            pp = 4 + (sub % 2)
            for ci in range(NEC):
                S.op("pe", "matmul", bank(pp),
                     xv(ci, T)[:, sub * 128:sub * 128 + 128],
                     wvp[:, ci * 512:(ci + 1) * 512],
                     start=(ci == 0), stop=(ci == NEC - 1),
                     wait=[frees.get(("bv", pp)), (dW.s, wv_dw[0])] if ci == 0 else (),
                     inc=cPE.inc() if ci == NEC - 1 else None)
            m_v = cPE.n
            kt = T * 4 + sub
            S.op("dve", "tensor_copy", V_tm[:, kt * 512:(kt + 1) * 512], bank(pp),
                 wait=[(cPE.s, m_v)], inc=cDVE.inc())
            frees[("bv", pp)] = (cDVE.s, cDVE.n)
        xb_free[T % 2] = (cPE.s, cPE.n)

    for i, T in enumerate((2, 3)):
        S.op("sync", "dma_start", out=cosb[i], in_=cos_t[:, T * 512:(T + 1) * 512],
             wait=[frees.get(("cos", i))], inc=dX.inc(16))
        S.op("sync", "dma_start", out=sinb[i], in_=sin_t[:, T * 512:(T + 1) * 512],
             inc=dX.inc(16))
    q_cs_dx = dX.n
    wp_free = {}
    for h in range(NH):
        hp = h % 2
        S.op("pool", "dma_start", out=wpA[hp], in_=wq_d[h],
             wait=[wp_free.get(hp)], inc=dW.inc(16))
        wq_dw = dW.n
        for t2 in range(2):
            pp, m_proj = do_proj(
                lambda ci, hp=hp: wpA[hp][:, ci * 128:(ci + 1) * 128],
                lambda ci, t2=t2: x_own[:, ci * OWN + t2 * 512: ci * OWN + t2 * 512 + 512],
                extra_wait=[(dW.s, wq_dw), (dX.s, q_cs_dx)])
            ctx = chain_front(pp, cosb[t2], sinb[t2], m_proj)
            projfree[pp] = (cDVE.s, ctx["m_qr"])
            _advance((ctx, qsc_s,
                      Q_fm[:, h * OWN + t2 * 512: h * OWN + t2 * 512 + 512]))
        wp_free[hp] = (cPE.s, cPE.n)
    _flush_all()

    S.barrier(CTRS)

    if "qkv" in taps:
        S.op("sync", "dma_start", out=dbg["k"], in_=K_fm, inc=dS.inc(16))
        S.op("sync", "dma_start", out=dbg["v"], in_=V_tm, inc=dS.inc(16))
        S.op("sync", "dma_start", out=dbg["q"], in_=Q_fm, inc=dS.inc(16))

    st_free = {}
    pt_free = {}
    th_free = {}
    pv_free = {}
    nrm_free = None
    z_read = None
    post = []

    for qc in range(2):
        blocks = _blocks(qc)
        groups = [blocks[i:i + 2] for i in range(0, 20, 2)]
        z_started = [False]
        nz_total = NH * 20
        nz = [0]
        flat = [(h, gi, grp) for h in range(NH) for gi, grp in enumerate(groups)]
        pv_cnt = {h: 0 for h in range(NH)}
        ginfo = {}

        def emit_pvz(idx, flat=flat, pv_cnt=pv_cnt, ginfo=ginfo,
                     z_started=z_started, nz=nz, nz_total=nz_total, qc=qc):
            h, gi, grp = flat[idx]
            g = h // 4
            hv = 4 + h % 2
            p, m_exp, m_tri = ginfo.pop(idx)
            for bi, (K0, jlo, jhi, tris) in enumerate(grp):
                kt = K0 // 128
                n0, n1 = jlo * 128, (jhi + 1) * 128
                rhs = pt[p][:, bi * 512 + n0: bi * 512 + n1]
                w = [(cACT.s, m_exp)]
                if tris:
                    w.append((cDVE.s, m_tri))
                if pv_cnt[h] == 0:
                    w.append(pv_free.get(hv))
                S.op("pe", "matmul", bank(hv)[:, n0:n1],
                     V_tm[:, kt * 512 + g * 128: kt * 512 + g * 128 + 128],
                     rhs, start=(pv_cnt[h] == 0), stop=(pv_cnt[h] == 19),
                     skip_group_check=True, wait=w)
                pv_cnt[h] += 1
                zw = []
                if not z_started[0]:
                    zw = [z_read] if z_read else []
                nz[0] += 1
                S.op("pe", "matmul", bank(7, parts=16)[:, n0:n1],
                     zsel[:, kt * 31 + (15 - h): kt * 31 + (15 - h) + 16],
                     rhs, start=(not z_started[0]), stop=(nz[0] == nz_total),
                     skip_group_check=True, wait=zw,
                     inc=cPE.inc())
                z_started[0] = True
                pt_free[p] = (cPE.s, cPE.n)
            if pv_cnt[h] == 20:
                S.op("dve", "tensor_copy",
                     attn[:, h * OWN + qc * 512: h * OWN + qc * 512 + 512],
                     bank(hv), wait=[(cPE.s, cPE.n)], inc=cDVE.inc())
                pv_free[hv] = (cDVE.s, cDVE.n)

        for idx, (h, gi, grp) in enumerate(flat):
            g = h // 4
            p = idx % 2
            p3 = idx % 3
            gsz = len(grp)
            a0 = grp[0][1] * 128
            a1 = (gsz - 1) * 512 + (grp[-1][2] + 1) * 128
            for bi, (K0, jlo, jhi, tris) in enumerate(grp):
                n0, n1 = jlo * 128, (jhi + 1) * 128
                w = [st_free.get(p)] if bi == 0 else ()
                S.op("pe", "matmul",
                     PS[:, (2 * p + bi) * 512 + n0: (2 * p + bi) * 512 + n1],
                     K_fm[:, g * HALO + K0: g * HALO + K0 + 128],
                     Q_fm[:, h * OWN + qc * 512 + n0: h * OWN + qc * 512 + n1],
                     start=True, stop=True, wait=w,
                     inc=cPE.inc() if bi == gsz - 1 else None)
            m_sc = cPE.n
            segs = []
            for bi, (K0, jlo, jhi, tris) in enumerate(grp):
                s0, s1 = bi * 512 + jlo * 128, bi * 512 + (jhi + 1) * 128
                if segs and segs[-1][1] == s0:
                    segs[-1][1] = s1
                else:
                    segs.append([s0, s1])
            for si, (s0, s1) in enumerate(segs):
                S.op("act", "activation", th[p3][:, s0:s1],
                     PS[:, 2 * p * 512 + s0: 2 * p * 512 + s1], AF.Tanh,
                     scale=1.0 / (np.sqrt(DH) * CAP),
                     wait=[(cPE.s, m_sc), th_free.get(p3)] if si == 0 else (),
                     inc=cACT.inc() if si == len(segs) - 1 else None)
            st_free[p] = (cACT.s, cACT.n)
            for si, (s0, s1) in enumerate(segs):
                S.op("act", "activation", pt[p3][:, s0:s1], th[p3][:, s0:s1],
                     AF.Exp, scale=CAP, bias=neg25,
                     wait=[pt_free.get(p3)] if si == 0 else (),
                     inc=cACT.inc() if si == len(segs) - 1 else None)
            m_exp = cACT.n
            th_free[p3] = (cACT.s, m_exp)
            m_tri = cDVE.n
            ntri = sum(len(t[3]) for t in grp)
            if ntri:
                cnt = 0
                for bi, (K0, jlo, jhi, tris) in enumerate(grp):
                    for (j, sign) in tris:
                        cnt += 1
                        sl = pt[p3][:, bi * 512 + j * 128: bi * 512 + j * 128 + 128]
                        tm = tri[:, 0:128] if sign > 0 else tri[:, 128:256]
                        S.op("dve", "tensor_mul", sl, sl, tm,
                             wait=[(cACT.s, m_exp)],
                             inc=cDVE.inc() if cnt == ntri else None)
                m_tri = cDVE.n
            ginfo[idx] = (p3, m_exp, m_tri)
            if idx >= 2:
                emit_pvz(idx - 2)
            if post:
                post.pop(0)()
        emit_pvz(len(flat) - 2)
        emit_pvz(len(flat) - 1)
        while post:
            post.pop(0)()
        m_z = cPE.n
        S.op("dve", "reciprocal", rzf, bank(7, parts=16), wait=[(cPE.s, m_z)])
        S.op("dve", "tensor_copy", rzhi, rzf)
        S.op("dve", "tensor_sub", rzlo, rzf, rzhi, inc=cDVE.inc())
        m_rz = cDVE.n
        z_read = (cDVE.s, m_rz)

        def mk_norm(h, qc=qc, m_rz=m_rz):
            def go():
                nonlocal nrm_free
                S.op("pe", "matmul", bank(6), rsel[:, h * 128:(h + 1) * 128],
                     rzhi, start=True, stop=False,
                     wait=[(cDVE.s, m_rz), nrm_free,
                           pv_free.get(4 + h % 2)])
                S.op("pe", "matmul", bank(6), rsel[:, h * 128:(h + 1) * 128],
                     rzlo, start=False, stop=True, inc=cPE.inc())
                m_r1 = cPE.n
                sl = attn[:, h * OWN + qc * 512: h * OWN + qc * 512 + 512]
                S.op("dve", "tensor_mul", sl, sl, bank(6),
                     wait=[(cPE.s, m_r1)], inc=cDVE.inc())
                nrm_free = (cDVE.s, cDVE.n)
            return go
        for h in range(NH):
            post.append(mk_norm(h))
    while post:
        post.pop(0)()

    S.barrier(CTRS)
    if "attn" in taps:
        S.op("sync", "dma_start", out=dbg["attn"], in_=attn, inc=dS.inc(16))

    def ln_stats_tail(sbank, qbank, m_sums, rows, bvec):
        meanv, varv, rstdv, mrv, hiv, lov, mhiv, mlov = rows
        rbv, mrbv, bb1, bb2 = bvec
        S.op("dve", "tensor_scalar_mul", meanv, bank(sbank, parts=1), 1.0 / E,
             wait=[(cPE.s, m_sums)])
        S.op("dve", "tensor_mul", varv, meanv, meanv)
        S.op("dve", "scalar_tensor_tensor", varv, bank(qbank, parts=1), 1.0 / E,
             varv, ALU.mult, ALU.subtract, inc=cDVE.inc())
        m_var = cDVE.n
        S.op("act", "activation", rstdv, varv, AF.Ln, bias=eps5[0:1, :],
             wait=[(cDVE.s, m_var)])
        S.op("act", "activation", rstdv, rstdv, AF.Exp, scale=-0.5,
             inc=cACT.inc())
        m_rstd = cACT.n
        S.op("dve", "tensor_mul", mrv, meanv, rstdv, wait=[(cACT.s, m_rstd)])
        S.op("dve", "tensor_copy", hiv, rstdv)
        S.op("dve", "tensor_sub", lov, rstdv, hiv)
        S.op("dve", "tensor_copy", mhiv, mrv)
        S.op("dve", "tensor_sub", mlov, mrv, mhiv, inc=cDVE.inc())
        m_rows2 = cDVE.n
        S.op("pe", "matmul", bank(bb1), ones_r, hiv, start=True, stop=False,
             wait=[(cDVE.s, m_rows2)])
        S.op("pe", "matmul", bank(bb1), ones_r, lov, start=False, stop=True)
        S.op("pe", "matmul", bank(bb2), ones_r, mhiv, start=True, stop=False)
        S.op("pe", "matmul", bank(bb2), ones_r, mlov, start=False, stop=True,
             inc=cPE.inc())
        m_bc = cPE.n
        S.op("dve", "tensor_copy", rbv, bank(bb1), wait=[(cPE.s, m_bc)])
        S.op("dve", "tensor_copy", mrbv, bank(bb2), inc=cDVE.inc())
        return cDVE.n

    wpc_free = {}
    xres_free = {}
    m_x2bf = {}
    gup_free = {}
    dwp_free = {}
    yB = [yC, view(153600, 128, NEC * 512, F32)]

    def c_stats(ce, qc):
        m_y = frees[("my", ce, qc)]
        S.op("pe", "matmul", bank(2 + 2 * qc, parts=1), ones_c,
             ybfr[2 * qc + ce % 2],
             start=(ce == 0), stop=(ce == NEC - 1),
             wait=[m_y])
        S.op("pe", "matmul", bank(3 + 2 * qc, parts=1), ones_c,
             sqc[2 * qc + ce % 2],
             start=(ce == 0), stop=(ce == NEC - 1),
             inc=cPE.inc())
        frees[("ybf", qc, ce % 2)] = (cPE.s, cPE.n)
        frees[("sqc", qc, ce % 2)] = (cPE.s, cPE.n)

    for ce in range(NEC):
        cp = ce % 2
        S.op("pool", "dma_start", out=wpC[cp], in_=wo_d[ce],
             wait=[wpc_free.get(cp)], inc=dW.inc(16))
        wo_dw = dW.n
        for qc in range(2):
            xr = xres[2 * qc + ce % 2]
            S.op("sync", "dma_start", out=xr,
                 in_=xt[ce * 128:(ce + 1) * 128, 1024 + qc * 512: 1536 + qc * 512],
                 wait=[xres_free.get((qc, ce % 2))], inc=dX.inc(16))
            xr_dx = dX.n
            for ch in range(NEC):
                S.op("pe", "matmul", bank(qc),
                     wpC[cp][:, ch * 128:(ch + 1) * 128],
                     attn[:, ch * OWN + qc * 512: ch * OWN + qc * 512 + 512],
                     start=(ch == 0), stop=(ch == NEC - 1),
                     wait=[(dW.s, wo_dw), frees.get(("cb", qc))] if ch == 0 else (),
                     inc=cPE.inc() if ch == NEC - 1 else None)
            m_wo = cPE.n
            yv = yB[qc][:, ce * 512:(ce + 1) * 512]
            S.op("dve", "tensor_add", yv, bank(qc), xr,
                 wait=[(cPE.s, m_wo), (dX.s, xr_dx)])
            frees[("cb", qc)] = (cDVE.s, cDVE.n + 1)
            S.op("dve", "tensor_copy", ybfr[2 * qc + ce % 2], yv,
                 wait=[frees.get(("ybf", qc, ce % 2))])
            S.op("dve", "tensor_mul", sqc[2 * qc + ce % 2], ybfr[2 * qc + ce % 2],
                 ybfr[2 * qc + ce % 2],
                 wait=[frees.get(("sqc", qc, ce % 2))], inc=cDVE.inc())
            frees[("my", ce, qc)] = (cDVE.s, cDVE.n)
            xres_free[(qc, ce % 2)] = (cDVE.s, cDVE.n)
            if ce >= 1:
                c_stats(ce - 1, qc)
        wpc_free[cp] = (cPE.s, cPE.n)
    for qc in range(2):
        c_stats(NEC - 1, qc)
    m_sums = cPE.n
    x2s_chunk = {}
    for qc in range(2):
        bb = (6, 7) if qc == 0 else (4, 5)
        m_bcast = ln_stats_tail(2 + 2 * qc, 3 + 2 * qc, m_sums,
                                (meanr, varr, rstdr, mrr, shi, slo, mhi, mlo),
                                (rbsb, mrbsb, bb[0], bb[1]))
        if qc == 0:
            frees["b6d"] = (cDVE.s, m_bcast)
            frees["b7d"] = (cDVE.s, m_bcast)
        for ce in range(NEC):
            yv = yB[qc][:, ce * 512:(ce + 1) * 512]
            S.op("dve", "scalar_tensor_tensor", yv, yv, 1.0, rbsb,
                 ALU.mult, ALU.mult, wait=[(cDVE.s, m_bcast)])
            S.op("dve", "tensor_sub", yv, yv, mrbsb, inc=cDVE.inc())
            m_x2 = cDVE.n
            S.op("act", "activation",
                 x2bf[:, ce * OWN + qc * 512: ce * OWN + qc * 512 + 512], yv,
                 AF.Copy, wait=[(cDVE.s, m_x2)], inc=cACT.inc())
            frees[("x2t", qc, ce)] = (cACT.s, cACT.n)
            S.op("sync", "dma_start",
                 out=x2scr[ce * 128:(ce + 1) * 128, qc * 512:(qc + 1) * 512],
                 in_=yv, wait=[frees[("x2t", qc, ce)]], inc=dS.inc(16))
            frees[("y0s", qc, ce)] = (dS.s, dS.n)
        m_x2bf[qc] = cDVE.n
        x2s_chunk[qc] = dS.n
    x2s_done = dS.n

    frees[("dg", 0)] = frees.get(("cb", 0))
    frees[("dg", 1)] = frees.get(("cb", 1))
    frees[("dg", 2)] = (cDVE.s, m_x2bf[0])
    frees[("dg", 3)] = (cDVE.s, m_x2bf[0])
    Y0_O = 120832
    for _i in range(4):
        hi_t = (GUP_O + (_i + 1) * 8192 - Y0_O + 2047) // 2048 - 1
        if hi_t >= 0:
            gup_free[_i] = frees[("y0s", 0, min(hi_t, NEC - 1))]
    gup_free["pe"] = (cPE.s, m_sums)
    dwp_free[0] = (cDVE.s, m_x2bf[1])
    dwp_free[1] = (cDVE.s, m_x2bf[1])
    frees["hfree"] = (dS.s, x2s_done)

    x2r_free = {}
    outs_free = {}
    m_xsb_c = {}
    for qc in range(2):
        for ce in range(NEC):
            S.op("pe", "matmul", bank(6, parts=1), ones_c,
                 x2bf[:, ce * OWN + qc * 512: ce * OWN + qc * 512 + 512],
                 start=(ce == 0), stop=(ce == NEC - 1),
                 wait=[frees.get(("x2t", qc, ce))] +
                      ([frees.get("b6d")] if ce == 0 else []),
                 inc=cPE.inc() if ce == NEC - 1 else None)
        m_xs = cPE.n
        S.op("dve", "tensor_scalar_mul", mean2, bank(6, parts=1), -128.0,
             wait=[(cPE.s, m_xs)])
        S.op("dve", "tensor_copy", shi2, mean2)
        S.op("dve", "tensor_sub", slo2, mean2, shi2, inc=cDVE.inc())
        m_hl = cDVE.n
        S.op("pe", "matmul", bank(7), ones_r, shi2, start=True, stop=False,
             wait=[(cDVE.s, m_hl), frees.get("b7d")])
        S.op("pe", "matmul", bank(7), ones_r, slo2, start=False, stop=True,
             inc=cPE.inc())
        m_bc = cPE.n
        S.op("dve", "tensor_copy", xsb[qc], bank(7),
             wait=[(cPE.s, m_bc)], inc=cDVE.inc())
        m_xsb_c[qc] = cDVE.n
        frees["b6d"] = (cDVE.s, m_xsb_c[qc])
        frees["b7d"] = (cDVE.s, m_xsb_c[qc])

    borrow_dw = {}
    for qc in range(2):
        m_xsb = m_xsb_c[qc]

        for ic in range(NIC):
            ip = (ic // 2) % 2
            sub = (ic % 2) * 2048
            borrow = (qc == 1 and ic < 4)
            if borrow:
                gpan = dwp[ic // 2][:, 0:4096]
                upan = dwp[ic // 2][:, 4096:8192]
            else:
                gpan = gup[ip]
                upan = gup[2 + ip]
            if ic % 2 == 0 and not borrow:
                gw_wait = [gup_free.get(ip), gup_free.get("pe")]
                uw_wait = [gup_free.get(2 + ip)]
                S.op("pool", "dma_start",
                     out=gpan.rearrange("p (g c) -> p g c", g=2),
                     in_=gw_d[ic:ic + 2].transpose([1, 0, 2]),
                     wait=gw_wait, inc=dW.inc(16))
                S.op("pool", "dma_start",
                     out=upan.rearrange("p (g c) -> p g c", g=2),
                     in_=uw_d[ic:ic + 2].transpose([1, 0, 2]),
                     wait=uw_wait, inc=dW.inc(16))
            w_dw = borrow_dw[ic // 2] if borrow else dW.n
            bg, bu = ic % 2, 2 + ic % 2
            for ch in range(NEC):
                w = list([(dW.s, w_dw), frees.get(("dg", bg)),
                          frees.get(("db", bg))] if ch == 0 else [])
                if ic == 0:
                    w.append(frees.get(("x2t", qc, ch)))
                S.op("pe", "matmul", bank(bg),
                     gpan[:, sub + ch * 128: sub + (ch + 1) * 128],
                     x2bf[:, ch * OWN + qc * 512: ch * OWN + qc * 512 + 512],
                     start=(ch == 0), stop=(ch == NEC - 1),
                     wait=w)
            for ch in range(NEC):
                S.op("pe", "matmul", bank(bu),
                     upan[:, sub + ch * 128: sub + (ch + 1) * 128],
                     x2bf[:, ch * OWN + qc * 512: ch * OWN + qc * 512 + 512],
                     start=(ch == 0), stop=(ch == NEC - 1),
                     wait=[frees.get(("dg", bu)),
                           frees.get(("db", bu))] if ch == 0 else (),
                     inc=cPE.inc() if ch == NEC - 1 else None)
            m_gu = cPE.n
            if borrow and ic % 2 == 1:
                dwp_free[ic // 2] = (cPE.s, m_gu)
            S.op("dve", "tensor_add", tsc1, bank(bg), xsb[qc],
                 wait=[(cPE.s, m_gu), (cDVE.s, m_xsb), frees.get("tsc1")])
            S.op("dve", "tensor_add", tsc2, bank(bu), xsb[qc],
                 wait=[frees.get("tsc2")], inc=cDVE.inc())
            m_t = cDVE.n
            frees[("dg", bg)] = (cDVE.s, m_t)
            frees[("dg", bu)] = (cDVE.s, m_t)
            if ic % 2 == 1 and not borrow:
                gup_free[ip] = (cPE.s, m_gu)
                gup_free[2 + ip] = (cPE.s, m_gu)
            S.op("act", "activation", sgs, tsc1, AF.Silu,
                 scale=gs_s[:, ic:ic + 1],
                 wait=[(cDVE.s, m_t), frees.get("sgs")], inc=cACT.inc())
            m_sg = cACT.n
            frees["tsc1"] = (cACT.s, m_sg)
            S.op("dve", "scalar_tensor_tensor", hS[:, ic * 512:(ic + 1) * 512],
                 tsc2, us_s[:, ic:ic + 1], sgs, ALU.mult, ALU.mult,
                 wait=[(cACT.s, m_sg), frees.get("hfree")], inc=cDVE.inc())
            frees["tsc2"] = (cDVE.s, cDVE.n)
            frees["sgs"] = (cDVE.s, cDVE.n)
            frees[("hdone", ic)] = (cDVE.s, cDVE.n)
            if ic >= 2:
                jc = ic - 2
                S.op("pe", "matmul", bank(6, parts=1), ones_c,
                     hS[:, jc * 512:(jc + 1) * 512],
                     start=(jc == 0), stop=False, skip_group_check=True,
                     wait=[frees.get(("hdone", jc)),
                           frees.get("b6d") if jc == 0 else None])

        for jc in (NIC - 2, NIC - 1):
            S.op("pe", "matmul", bank(6, parts=1), ones_c,
                 hS[:, jc * 512:(jc + 1) * 512],
                 start=False, stop=(jc == NIC - 1), skip_group_check=True,
                 wait=[frees.get(("hdone", jc))],
                 inc=cPE.inc() if jc == NIC - 1 else None)
        m_hs = cPE.n
        S.op("dve", "tensor_scalar_mul", mean2, bank(6, parts=1), -128.0,
             wait=[(cPE.s, m_hs)])
        S.op("dve", "tensor_copy", shi2, mean2)
        S.op("dve", "tensor_sub", slo2, mean2, shi2, inc=cDVE.inc())
        m_hl2 = cDVE.n
        S.op("pe", "matmul", bank(7), ones_r, shi2, start=True, stop=False,
             wait=[(cDVE.s, m_hl2), frees.get("b7d")])
        S.op("pe", "matmul", bank(7), ones_r, slo2, start=False, stop=True,
             inc=cPE.inc())
        m_bc2 = cPE.n
        S.op("dve", "tensor_copy", xsb2, bank(7),
             wait=[(cPE.s, m_bc2), frees.get("xsb2")], inc=cDVE.inc())
        m_xsb2 = cDVE.n
        frees["b6d"] = (cDVE.s, m_xsb2)
        frees["b7d"] = (cDVE.s, m_xsb2)

        def d_stats(ce):
            dp = ce % 2
            S.op("pe", "matmul", bank(2, parts=1), ones_c, y2bfr[dp],
                 start=(ce == 0), stop=(ce == NEC - 1),
                 wait=[frees[("my2", ce)]] + ([frees.get("lnb2")] if ce == 0 else []))
            S.op("pe", "matmul", bank(3, parts=1), ones_c, sq2[dp],
                 start=(ce == 0), stop=(ce == NEC - 1),
                 inc=cPE.inc())
            frees[("y2bf", dp)] = (cPE.s, cPE.n)
            frees[("sq2", dp)] = (cPE.s, cPE.n)

        for ce in range(NEC):
            dp = ce % 2
            S.op("pool", "dma_start", out=dwp[dp], in_=dw_d[ce],
                 wait=[dwp_free.get(dp)], inc=dW.inc(16))
            d_dw = dW.n
            S.op("sync", "dma_start", out=x2r[dp],
                 in_=x2scr[ce * 128:(ce + 1) * 128, qc * 512:(qc + 1) * 512],
                 wait=[(dS.s, x2s_done), x2r_free.get(dp)], inc=dX.inc(16))
            d_dx = dX.n
            bp = ce % 2
            for ic in range(NIC):
                S.op("pe", "matmul", bank(bp),
                     dwp[dp][:, ic * 128:(ic + 1) * 128],
                     hS[:, ic * 512:(ic + 1) * 512],
                     start=(ic == 0), stop=(ic == NIC - 1),
                     wait=[(dW.s, d_dw), frees.get(("db", bp)),
                           frees.get(("dg", bp))] if ic == 0 else (),
                     inc=cPE.inc() if ic == NIC - 1 else None)
            m_dn = cPE.n
            dwp_free[dp] = (cPE.s, m_dn)
            yv = y2t(ce)
            S.op("dve", "tensor_add", tsc1, bank(bp), xsb2,
                 wait=[(cPE.s, m_dn), (cDVE.s, m_xsb2), frees.get("tsc1")])
            S.op("dve", "scalar_tensor_tensor", yv, tsc1, ds_s[:, ce:ce + 1],
                 x2r[dp], ALU.mult, ALU.add,
                 wait=[(dX.s, d_dx), frees.get(("y2", qc, ce))])
            frees[("db", bp)] = (cDVE.s, cDVE.n + 1)
            frees["tsc1"] = (cDVE.s, cDVE.n + 1)
            S.op("dve", "tensor_copy", y2bfr[dp], yv,
                 wait=[frees.get(("y2bf", dp))])
            S.op("dve", "tensor_mul", sq2[dp], y2bfr[dp], y2bfr[dp],
                 wait=[frees.get(("sq2", dp))], inc=cDVE.inc())
            m_y2 = cDVE.n
            frees[("my2", ce)] = (cDVE.s, m_y2)
            x2r_free[dp] = (cDVE.s, m_y2)
            if ce >= 1:
                d_stats(ce - 1)
        d_stats(NEC - 1)
        m_sums2 = cPE.n
        m_bcast2 = ln_stats_tail(2, 3, m_sums2,
                                 (mean2, var2, rstd2, mr2, shi2, slo2, mhi2, mlo2),
                                 (rb2, mrb2, 4, 5))
        frees["lnb2"] = (cDVE.s, m_bcast2)
        frees["xsb2"] = (cDVE.s, m_bcast2)
        if qc == 0:
            for pair in range(2):
                S.op("pool", "dma_start",
                     out=dwp[pair][:, 0:4096].rearrange("p (g c) -> p g c", g=2),
                     in_=gw_d[2 * pair:2 * pair + 2].transpose([1, 0, 2]),
                     wait=[dwp_free.get(pair)], inc=dW.inc(16))
                S.op("pool", "dma_start",
                     out=dwp[pair][:, 4096:8192].rearrange("p (g c) -> p g c", g=2),
                     in_=uw_d[2 * pair:2 * pair + 2].transpose([1, 0, 2]),
                     inc=dW.inc(16))
                borrow_dw[pair] = dW.n
        for ce in range(NEC):
            op = ce % 2
            yv = y2t(ce)
            eng = "pool" if ce % 2 == 1 else "dve"
            ctr = cPOOL if eng == "pool" else cDVE
            S.op(eng, "tensor_mul", outs[op], yv, rb2,
                 wait=[(cDVE.s, m_bcast2), outs_free.get(op)])
            S.op(eng, "tensor_sub", outs[op], outs[op], mrb2,
                 inc=ctr.inc())
            m_o = ctr.n
            m_o_eng = ctr.s
            frees[("y2", qc + 1, ce)] = (m_o_eng, m_o)
            if ce % 4 == 3:
                gup_free[ce // 4] = (m_o_eng, m_o)
            S.op("sync", "dma_start",
                 out=out_d[ce * 128:(ce + 1) * 128, qc * 512:(qc + 1) * 512],
                 in_=outs[op], wait=[(m_o_eng, m_o)], inc=dS.inc(16))
            outs_free[op] = (dS.s, dS.n)
        frees["hfree"] = (cPE.s, cPE.n)
        frees[("dg", 2)] = (cDVE.s, cDVE.n)
        frees[("dg", 3)] = (cDVE.s, cDVE.n)

    S.raw("sync", lambda e, s=dS.s, v=dS.n: e.wait_ge(s, v))

    with nc.Block() as blk:
        blk.tensor(lambda e: [f(e) for f in S.q["pe"]])
        blk.scalar(lambda e: [f(e) for f in S.q["act"]])
        blk.vector(lambda e: [f(e) for f in S.q["dve"]])
        blk.gpsimd(lambda e: [f(e) for f in S.q["pool"]])
        blk.sync(lambda e: [f(e) for f in S.q["sync"]])
    return nc


def _rope_tables():
    half = DH // 2
    e = np.arange(half, dtype=np.float32) / np.float32(half)
    inv = (np.float32(1.0) / np.power(np.float32(10000.0), e)).astype(np.float32)
    ang = (np.arange(SEQ, dtype=np.float32)[:, None] * inv[None, :]).astype(np.float32)
    ang = np.concatenate([ang, ang], axis=1)
    return (np.ascontiguousarray(np.cos(ang).T.astype(np.float32)),
            np.ascontiguousarray(np.sin(ang).T.astype(np.float32)))


def _prep(inputs):
    x = np.asarray(inputs["L0_attn_hidden_states"], dtype=np.float32)[0]
    xt_full = np.ascontiguousarray(x.T)
    wq = np.asarray(inputs["L0_attn_wq"], dtype=np.float32)
    wk = np.asarray(inputs["L0_attn_wk"], dtype=np.float32)
    wv = np.asarray(inputs["L0_attn_wv"], dtype=np.float32)
    wo = np.asarray(inputs["L0_attn_wo"], dtype=np.float32)
    qsc = np.asarray(inputs["L0_attn_query_scale"], dtype=np.float32).reshape(128, 1)
    ksc = np.asarray(inputs["L0_attn_key_scale"], dtype=np.float32).reshape(128, 1)
    def u8(a):
        a = np.ascontiguousarray(np.asarray(a, dtype=np.int32))
        return np.ascontiguousarray(a.view(np.uint8).reshape(a.shape + (4,))[..., 0])
    gw = u8(inputs["L0_ffn_gate_w"])
    uw = u8(inputs["L0_ffn_up_w"])
    dw = u8(inputs["L0_ffn_down_w"])
    gsc = np.asarray(inputs["L0_ffn_gate_scale"], dtype=np.float32)
    usc = np.asarray(inputs["L0_ffn_up_scale"], dtype=np.float32)
    dsc = np.asarray(inputs["L0_ffn_down_scale"], dtype=np.float32)

    shared = {
        "wq_t": np.ascontiguousarray(wq.reshape(NEC, 128, NH, 128).transpose(2, 1, 0, 3).reshape(NH, 128, NEC * 128)),
        "wk_t": np.ascontiguousarray(wk.reshape(NEC, 128, NKV, 128).transpose(2, 1, 0, 3).reshape(NKV, 128, NEC * 128)),
        "wv_t": np.ascontiguousarray(wv.reshape(NEC, 128, 512)),
        "wo_t": np.ascontiguousarray(wo.reshape(NEC, 128, NEC, 128).transpose(2, 1, 0, 3).reshape(NEC, 128, NEC * 128)),
        "gw_t": np.ascontiguousarray(gw.reshape(NIC, 128, NEC, 128).transpose(0, 3, 2, 1).reshape(NIC, 128, NEC * 128)),
        "uw_t": np.ascontiguousarray(uw.reshape(NIC, 128, NEC, 128).transpose(0, 3, 2, 1).reshape(NIC, 128, NEC * 128)),
        "dw_t": np.ascontiguousarray(dw.reshape(NEC, 128, NIC, 128).transpose(0, 3, 2, 1).reshape(NEC, 128, NIC * 128)),
        "gs_t": np.ascontiguousarray(gsc.reshape(NIC, 128).T),
        "us_t": np.ascontiguousarray(usc.reshape(NIC, 128).T),
        "ds_t": np.ascontiguousarray(dsc.reshape(NEC, 128).T),
        "qsc": qsc, "ksc": ksc,
    }
    kk = np.arange(128)[:, None]
    qq = np.arange(128)[None, :]
    tpos = (kk <= qq).astype(ml_dtypes.bfloat16)
    tneg = (kk >= qq).astype(ml_dtypes.bfloat16)
    shared["tri"] = np.ascontiguousarray(np.concatenate([tpos, tneg], axis=1))
    rs = np.zeros((16, NH * 128), dtype=ml_dtypes.bfloat16)
    for h in range(NH):
        rs[h, h * 128:(h + 1) * 128] = 1
    shared["rsel"] = rs

    cosT, sinT = _rope_tables()
    maps = []
    for cid in range(NCORE):
        hs = cid * OWN - 1024
        xt_c = np.zeros((E, HALO), np.float32)
        lo, hi = max(0, hs), min(SEQ, hs + HALO)
        xt_c[:, lo - hs:hi - hs] = xt_full[:, lo:hi]
        pos = np.clip(np.arange(hs, hs + HALO), 0, SEQ - 1)
        zs = np.zeros((128, NKT * 31), dtype=ml_dtypes.bfloat16)
        valid = ((np.arange(hs, hs + HALO) >= 0) & (np.arange(hs, hs + HALO) < SEQ))
        for t in range(NKT):
            zs[:, t * 31 + 15] = valid[t * 128:(t + 1) * 128].astype(ml_dtypes.bfloat16)
        m = dict(shared)
        m["xt"] = xt_c
        m["cos_t"] = np.ascontiguousarray(cosT[:, pos])
        m["sin_t"] = np.ascontiguousarray(sinT[:, pos])
        m["zsel"] = zs
        maps.append(m)
    return maps


_CACHE = {}


def _get_program(taps=()):
    key = tuple(sorted(taps))
    if key not in _CACHE:
        _CACHE[key] = build_program(taps)
    return _CACHE[key]


def kernel(**inputs):
    nc = _get_program()
    maps = _prep(inputs)
    res = run_bass_kernel_spmd(nc, maps, core_ids=list(range(NCORE)))
    out = np.concatenate([r["out_t"] for r in res.results], axis=1)
    return np.ascontiguousarray(out.T).reshape(1, SEQ, E)
